# version 04
# baseline (speedup 1.0000x reference)
import numpy as np
import ml_dtypes
import concourse.bass as bass
import concourse.mybir as mybir
from concourse.bass_utils import run_bass_kernel_spmd

F32 = mybir.dt.float32
BF16 = mybir.dt.bfloat16
AF = mybir.ActivationFunctionType
OP = mybir.AluOpType
AX = mybir.AxisListType

L = 2048
DM = 2048
DIN = 4096
NH = 64
HD = 64
DST = 128
DXBC = 4352
NGX = 34
NGA1 = 35
NBLK = 4
TB = 512
CH = 64
NCHB = TB // CH
DFF = 5632
NJ = 44
NE = 8
RK = 16
SCALING = 2.0

BFNP = ml_dtypes.bfloat16


def _stage_shared(ins):
    f32 = np.float32
    s = {}
    w = np.asarray(ins["L0_attn_in_proj_w"], f32)
    w1t = np.ascontiguousarray(w[4096:8512].T)
    w1tp = np.zeros((2048, 4480), f32)
    w1tp[:, :4416] = w1t
    s["w1c"] = np.ascontiguousarray(
        w1tp.reshape(16, 128, 35, 128).transpose(2, 1, 0, 3))
    s["wzt"] = np.ascontiguousarray(w[0:4096].T.reshape(16, 128, 4096))
    wo = np.asarray(ins["L0_attn_out_proj_w"], f32)
    s["wot"] = np.ascontiguousarray(wo.T.reshape(32, 128, 2048))
    cw = np.asarray(ins["L0_attn_conv_w"], f32).reshape(DXBC, 4)
    cwd = np.zeros((34, 4, 128, 128), f32)
    gi = np.arange(128)
    for g in range(34):
        for j in range(4):
            cwd[g, j, gi, gi] = cw[g * 128:(g + 1) * 128, j]
    s["cwdiag"] = np.ascontiguousarray(cwd.transpose(0, 2, 1, 3))
    s["convb"] = np.ascontiguousarray(
        np.asarray(ins["L0_attn_conv_b"], f32).reshape(34, 128).T)
    s["alog"] = np.asarray(ins["L0_attn_A_log"], f32).reshape(NH, 1)
    s["dvec"] = np.asarray(ins["L0_attn_D"], f32).reshape(NH, 1)
    s["dtb"] = np.asarray(ins["L0_attn_dt_bias"], f32).reshape(NH, 1)
    s["nw"] = np.ascontiguousarray(
        np.asarray(ins["L0_attn_norm_w"], f32).reshape(32, 128).T)

    wg = np.asarray(ins["L0_ffn_Wg"], f32)
    wu = np.asarray(ins["L0_ffn_Wu"], f32)
    wd = np.asarray(ins["L0_ffn_Wd"], f32)
    s["wgc"] = np.ascontiguousarray(
        wg.T.reshape(16, 128, 44, 128).transpose(2, 1, 0, 3))
    s["wuc"] = np.ascontiguousarray(
        wu.T.reshape(16, 128, 44, 128).transpose(2, 1, 0, 3))
    s["wdc"] = np.ascontiguousarray(
        wd.T.reshape(44, 128, 4, 512).transpose(2, 1, 0, 3))
    s["wrt"] = np.ascontiguousarray(
        np.asarray(ins["L0_ffn_Wr"], f32).T.reshape(16, 128, 8))
    ag = np.asarray(ins["L0_ffn_Ag"], f32)
    au = np.asarray(ins["L0_ffn_Au"], f32)
    s["agt"] = np.ascontiguousarray(ag.transpose(2, 0, 1).reshape(16, 128, 128))
    s["aut"] = np.ascontiguousarray(au.transpose(2, 0, 1).reshape(16, 128, 128))
    s["bgt"] = np.ascontiguousarray(np.asarray(ins["L0_ffn_Bg"], f32).transpose(0, 2, 1))
    s["but"] = np.ascontiguousarray(np.asarray(ins["L0_ffn_Bu"], f32).transpose(0, 2, 1))
    ad = np.asarray(ins["L0_ffn_Ad"], f32)
    s["adt"] = np.ascontiguousarray(ad.transpose(0, 2, 1).reshape(8, 44, 128, 16))
    s["bdt"] = np.ascontiguousarray(np.asarray(ins["L0_ffn_Bd"], f32).transpose(0, 2, 1))

    s["identf"] = np.eye(128, dtype=f32)
    tri = np.where(np.arange(CH)[:, None] <= np.arange(CH)[None, :], 0.0, -1e5)
    s["mtri"] = np.ascontiguousarray(np.tile(tri, (2, 1)).astype(f32))
    s["mres"] = np.ascontiguousarray(
        np.concatenate([np.zeros((CH, 1), f32), np.ones((CH, CH - 1), f32)], 1))
    selp = np.zeros((32, 64, 128), f32)
    for hp in range(32):
        selp[hp, 2 * hp, 0:64] = 1.0
        selp[hp, 2 * hp + 1, 64:128] = 1.0
    s["selp"] = selp
    i64x = np.zeros((128, 128), f32)
    i64x[0:64, 0:64] = np.eye(64)
    i64x[64:128, 64:128] = np.eye(64)
    s["i64x"] = i64x
    s["ones1"] = np.ones((1, 128), f32)
    sel8 = np.zeros((8, 128), f32)
    for e in range(8):
        sel8[e, 16 * e:16 * e + 16] = 1.0
    s["sel8"] = sel8
    s["zz16"] = np.zeros((RK, DFF), f32)
    return s


class B:
    ENGS = ("tensor", "vector", "scalar", "gpsimd", "sync")

    def __init__(self, nc):
        self.nc = nc
        self.q = {e: [] for e in self.ENGS}
        self.cnt = {}

    def on(self, eng, fn, inc=None, n=1):
        if inc is not None:
            self.cnt[inc] = self.cnt.get(inc, 0) + n
        self.q[eng].append((fn, inc, n))

    def dma(self, eng, out, in_, inc):
        self.on(eng, (lambda e, o=out, i=in_: e.dma_start(out=o, in_=i)), inc, 16)
        return self.cnt[inc]

    def wait(self, eng, sem, val=None):
        v = self.cnt.get(sem, 0) if val is None else val
        assert v <= self.cnt.get(sem, 0), "wait on future semaphore value"
        if v > 0:
            self.q[eng].append(((lambda e, s=sem, vv=v: e.wait_ge(s, vv)), None, 0))

    def barrier(self, sems):
        for e in self.ENGS:
            self.q[e].append(((lambda en: en.drain()), None, 0))
            self.on(e, (lambda en: None), sems[e], 1)
        for e in self.ENGS:
            for e2 in self.ENGS:
                if e2 != e:
                    self.wait(e, sems[e2])

    def finalize(self, block):
        for e in self.ENGS:
            ops = list(self.q[e])

            def mk(ops):
                def run(eng):
                    for fn, inc, n in ops:
                        r = fn(eng)
                        if inc is not None:
                            if r is None:
                                eng.sem_inc(inc, n)
                            else:
                                r.then_inc(inc, n)
                return run
            getattr(block, e)(mk(ops))


def build(dump=(), ssd_stage=99):
    nc = bass.Bass("TRN2", target_bir_lowering=False)
    dump = set(dump)

    def din(name, shape, dt=F32):
        return nc.dram_tensor(name, list(shape), dt, kind="ExternalInput").ap()

    x_in = din("x", [L, DM])
    w1c = din("w1c", [NGA1, 128, 16, 128])
    wzt = din("wzt", [16, 128, 4096])
    wot = din("wot", [32, 128, 2048])
    cwdiag_i = din("cwdiag", [34, 128, 4, 128])
    convb_i = din("convb", [128, 34])
    alog_i = din("alog", [NH, 1])
    dvec_i = din("dvec", [NH, 1])
    dtb_i = din("dtb", [NH, 1])
    nw_i = din("nw", [128, 32])
    wgc = din("wgc", [NJ, 128, 16, 128])
    wuc = din("wuc", [NJ, 128, 16, 128])
    wdc = din("wdc", [4, 128, NJ, 512])
    wrt = din("wrt", [16, 128, 8])
    agt_i = din("agt", [16, 128, 128])
    aut_i = din("aut", [16, 128, 128])
    bgt_i = din("bgt", [NE, RK, DFF])
    but_i = din("but", [NE, RK, DFF])
    adt_i = din("adt", [NE, NJ, 128, RK])
    bdt_i = din("bdt", [NE, RK, DM])
    identf_i = din("identf", [128, 128])
    mtri_i = din("mtri", [128, CH])
    mres_i = din("mres", [CH, CH])
    selp_i = din("selp", [32, 64, 128])
    i64x_i = din("i64x", [128, 128])
    ones1_i = din("ones1", [1, 128])
    sel8_i = din("sel8", [8, 128])
    zz16_i = din("zz16", [RK, DFF])

    out_t = nc.dram_tensor("out", [L, DM], F32, kind="ExternalOutput").ap()
    ysc = nc.dram_tensor("ysc", [L, DIN], BF16, kind="Internal").ap()
    ynsc = nc.dram_tensor("ynsc", [L, DIN], BF16, kind="Internal").ap()
    x1sc = nc.dram_tensor("x1sc", [L, DM], BF16, kind="Internal").ap()
    xfsc = nc.dram_tensor("xfsc", [16, 128, 16, 128], BF16, kind="Internal").ap()
    wesc = nc.dram_tensor("wesc", [8, L], BF16, kind="Internal").ap()

    dump_specs = {
        "xn": ([NBLK, 128, 16, TB], BF16),
        "xnt0": ([128, DM], BF16),
        "stat0": ([128, 8], F32),
        "scr0": ([128, DM], F32),
        "dtraw": ([NBLK, CH, TB], F32),
        "raw": ([NBLK, 128, NGX, TB], BF16),
        "cvo": ([NBLK, 128, NGX, TB], BF16),
        "cumA": ([NBLK, CH, TB], F32),
        "dtsp": ([NBLK, CH, TB], F32),
        "wt": ([NBLK, 128, 32, NCHB * CH], BF16),
        "xdtp": ([NBLK, NCHB // 2, 128, DIN], BF16),
        "srun": ([NBLK, 128, DIN], BF16),
        "y": ([L, DIN], BF16),
        "x1": ([L, DM], BF16),
        "yn": ([L, DIN], BF16),
        "sz0": ([128, DIN], BF16),
        "yg0": ([128, DIN], BF16),
        "we0": ([8, 512], BF16),
        "tew0": ([128, 512], BF16),
        "hb0": ([128, 512], BF16),
        "lg0": ([128, 8], F32),
        "xfh0": ([128, 4, 16, 128], BF16),
        "gate0q": ([128, 512], BF16),
        "wet0": ([128, 8], BF16),
        "st0": ([128, 8], F32),
        "wt0": ([128, 32, 256], BF16),
        "sr0": ([128, DIN], BF16),
        "ar0": ([2, 128, DIN], BF16),
        "xd0": ([2, 128, DIN], BF16),
        "vall0": ([128, NCHB, 32], F32),
        "vall": ([NBLK, 128, NCHB, 32], F32),
        "lam": ([NBLK, 128, NCHB, CH], F32),
    }
    dump_t = {k: nc.dram_tensor("d_" + k, list(sh), dt, kind="ExternalOutput").ap()
              for k, (sh, dt) in dump_specs.items() if k in dump}

    b = B(nc)
    ctx = []

    def sb(name, shape, dt):
        t = nc.sbuf_tensor(name, list(shape), dt)
        h = t.__enter__()
        ctx.append(t)
        return h

    def psum(name, shape, dt=F32):
        t = nc.psum_tensor(name, list(shape), dt)
        h = t.__enter__()
        ctx.append(t)
        return h

    def sem(name):
        t = nc.semaphore(name)
        h = t.__enter__()
        ctx.append(t)
        return h

    sLx = sem("sLx")
    sLw = sem("sLw")
    sLs = sem("sLs")
    sT = sem("sT")
    sV = sem("sV")
    sS = sem("sS")
    sbar = {e: sem("bar_" + e) for e in B.ENGS}

    def BAR():
        b.barrier(sbar)

    identb = sb("identb", [128, 128], BF16)
    identf = sb("identf_s", [128, 128], F32)
    convb = sb("convb_s", [128, 34], F32)
    alog = sb("alog_s", [NH, 1], F32)
    ealog = sb("ealog_s", [NH, 1], F32)
    dvec = sb("dvec_s", [NH, 1], F32)
    dtbn = sb("dtbn_s", [NH, 1], F32)
    nw = sb("nw_s", [128, 32], F32)
    mtri = sb("mtri_s", [128, CH], F32)
    mres = sb("mres_s", [CH, CH], F32)
    selp = sb("selp_s", [64, 32, 128], F32)
    i64x = sb("i64x_s", [128, 128], BF16)
    ones1 = sb("ones1_s", [1, 128], F32)
    sel8 = sb("sel8_s", [8, 128], BF16)
    zcol = sb("zcol", [CH, 1], F32)
    drep = sb("drep", [128, NH], BF16)
    drow = sb("drow", [1, NH], F32)

    xnt = sb("xnt", [128, DM], BF16)
    tails = sb("tails", [128, NGX, 3], BF16)
    xtm = sb("xtm", [128, DM], F32)
    scr1 = sb("scr1", [128, 1024], F32)
    stat = sb("stat", [128, 8], F32)
    srun = sb("srun", [128, DIN], BF16)

    AR_ELE = 84000
    UB = sb("UB", [128, AR_ELE], BF16)

    class Arena:
        def __init__(self):
            self.off = 0

        def take(self, shape, dt):
            esz = 2 if dt == BF16 else 4
            n = int(np.prod(shape[1:])) * esz
            off = (self.off + 3) & ~3
            self.off = off + n
            assert self.off <= AR_ELE * 2, f"arena overflow: {self.off}"
            ap = UB[:, off // 2:(off + n) // 2]
            if dt != BF16:
                ap = ap.bitcast(dt)
            if len(shape) == 2:
                pass
            elif len(shape) == 3:
                ap = ap.rearrange("p (a b) -> p a b", a=shape[1])
            elif len(shape) == 4:
                ap = ap.rearrange("p (a b c) -> p a b c", a=shape[1], b=shape[2])
            else:
                raise AssertionError(shape)
            if shape[0] != 128:
                ap = ap[0:shape[0]]
            return ap

    arena = Arena()
    A = arena.take
    xns = A([128, 16, TB], BF16)
    rawr = A([128, 2, TB + 3], BF16)
    dtraw = A([CH, TB], F32)
    cvo = A([128, NGX, TB], BF16)
    w1sb = A([128, 2, 2048], BF16)
    dtsp = A([CH, TB], F32)
    adisc = A([CH, TB], F32)
    cumA = A([CH, TB], F32)
    ends = A([CH, NCHB], F32)
    decay = A([CH, TB], F32)
    sdo = A([CH, TB], F32)
    erow8 = A([1, TB], F32)
    mres512 = A([CH, TB], F32)
    lamrep = A([128, NCHB, CH], F32)
    sdoTc = A([64, NCHB, CH], BF16)
    cumATs = A([128, NCHB, CH], F32)
    vall = A([128, NCHB, 32], F32)
    dtT = A([128, NCHB // 2, CH], BF16)
    decT = A([128, NCHB // 2, CH], BF16)
    wt = A([128, 32, 4 * CH], BF16)
    wtmp = A([128, 4 * CH], F32)
    wexp = A([128, 4 * CH], BF16)
    b2 = A([128, 2, 128], BF16)
    xdt = A([128, DIN], BF16)
    xdd = A([128, DIN], BF16)
    xdv = A([128, DIN], BF16)
    arT = A([128, DIN], BF16)
    arB = A([128, DIN], BF16)
    btok = A([128, 4, 128], BF16)
    srt = A([128, 512], BF16)
    btstage = A([CH, 128], BF16)
    yacc = A([64, DIN], BF16)
    ytile = A([64, DIN], BF16)
    cwg = A([128, 2, 512], BF16)
    a_end = arena.off

    pBig = psum("pBig", [128, 8, 512])
    pA = pBig[:, 0:2, :]
    pT = pBig[:, 2:4, :].bitcast(BF16).rearrange("p a b -> p (a b)")
    pW = pBig[:, 4, :]
    pSc = pBig[:, 5, :]
    pY = pBig[:, 6, :]
    pY2 = pBig[:, 7, :]

    b.dma("sync", identf[:], identf_i, sLx)
    b.dma("gpsimd", identb[:], identf_i, sLw)
    b.dma("sync", convb[:], convb_i, sLx)
    b.dma("sync", alog[:], alog_i, sLx)
    b.dma("sync", dvec[:], dvec_i, sLx)
    b.dma("sync", mtri[:], mtri_i, sLx)
    b.dma("sync", mres[:], mres_i, sLx)
    b.dma("sync", nw[:], nw_i, sLx)
    b.dma("sync", ones1[:], ones1_i, sLx)
    b.dma("gpsimd", sel8[:], sel8_i, sLw)
    b.dma("gpsimd", i64x[:], i64x_i, sLw)
    for hp in range(32):
        b.dma("sync", selp[:, hp, :], selp_i[hp], sLx)
    b.dma("sync", dtbn[:], dtb_i, sLx)
    b.wait("vector", sLx)
    b.wait("vector", sLw)
    b.on("vector", lambda e: e.memset(zcol[:], 0.0))
    b.on("vector", lambda e: e.tensor_scalar(dtbn[:], dtbn[:], -1.0, None, OP.mult), sV)
    b.wait("scalar", sLx)
    b.on("scalar", lambda e: e.activation(ealog[:], alog[:], AF.Exp), sS)
    b.wait("tensor", sLx)
    b.wait("tensor", sLw)
    b.on("tensor", lambda e: e.transpose(pW[0:1, 0:NH], dvec[:], identf[0:NH, 0:NH]), sT)
    b.wait("scalar", sT)
    b.on("scalar", lambda e: e.copy(drow[:], pW[0:1, 0:NH]), sS)
    b.wait("tensor", sS)
    b.on("tensor", lambda e: e.matmul(pY[:, 0:NH], ones1[:], drow[:],
                                      start=True, stop=True), sT)
    b.wait("vector", sT)
    b.on("vector", lambda e: e.tensor_copy(drep[:], pY[:, 0:NH]), sV)
    b.on("vector", lambda e: e.memset(tails[:], 0.0), sV)
    b.on("vector", lambda e: e.memset(btok[:], 0.0), sV)
    b.on("vector", lambda e: e.memset(arT[:], 0.0), sV)
    b.on("vector", lambda e: e.memset(arB[:], 0.0), sV)
    b.on("vector", lambda e: e.memset(srun[:], 0.0), sV)
    b.on("vector", lambda e: e.memset(mres512[:], 1.0))
    b.on("vector", lambda e: e.memset(
        mres512.rearrange("h (c l) -> h c l", c=NCHB)[:, :, 0:1], 0.0), sV)
    BAR()

    def DUMP(name, dst, src):
        if name not in dump_t:
            return
        for e2 in ("tensor", "vector", "scalar"):
            pass
        b.wait("gpsimd", sT)
        b.wait("gpsimd", sV)
        b.wait("gpsimd", sS)
        b.dma("gpsimd", dst, src, sLw)

    for blk in range(NBLK):
        t0 = blk * TB
        for i in range(4):
            b.wait("sync", sV)
            xrdy = b.dma("sync", xtm[:], x_in[t0 + i * 128: t0 + (i + 1) * 128, :], sLx)
            b.wait("scalar", sLx, xrdy)
            b.wait("vector", sLx, xrdy)
            b.on("scalar", lambda e: e.activation(
                scr1[:], xtm[:, 0:1024], AF.Square), sS)
            b.on("vector", lambda e: e.reduce_sum(
                stat[:, 0:1], xtm[:], axis=AX.X))
            b.wait("vector", sS)
            b.on("vector", lambda e: e.reduce_sum(
                stat[:, 1:2], scr1[:], axis=AX.X), sV)
            b.wait("scalar", sV)
            b.on("scalar", lambda e: e.activation(
                scr1[:], xtm[:, 1024:2048], AF.Square), sS)
            b.wait("vector", sS)
            b.on("vector", lambda e: e.reduce_sum(
                stat[:, 6:7], scr1[:], axis=AX.X))
            b.on("vector", lambda e: e.tensor_scalar(
                stat[:, 2:3], stat[:, 0:1], 1.0 / DM, None, OP.mult))
            b.on("vector", lambda e: e.tensor_tensor(
                stat[:, 1:2], stat[:, 1:2], stat[:, 6:7], OP.add))
            b.on("vector", lambda e: e.tensor_tensor(
                stat[:, 4:5], stat[:, 2:3], stat[:, 2:3], OP.mult))
            b.on("vector", lambda e: e.tensor_scalar(
                stat[:, 3:4], stat[:, 1:2], 1.0 / DM, None, OP.mult))
            b.on("vector", lambda e: e.tensor_copy(scr1[:, 40:41], scr1[:, 41:42]))
            b.on("vector", lambda e: e.tensor_scalar(
                stat[:, 5:6], stat[:, 3:4], stat[:, 4:5], 1e-5,
                OP.subtract, OP.add), sV)
            b.wait("scalar", sV)
            b.on("scalar", lambda e: e.activation(
                stat[:, 6:7], stat[:, 5:6], AF.Sqrt), sS)
            b.wait("vector", sS)
            b.on("vector", lambda e: e.reciprocal(stat[:, 7:8], stat[:, 6:7]))
            b.wait("vector", sT)
            b.on("vector", lambda e: e.tensor_scalar(
                xnt[:], xtm[:], stat[:, 2:3], stat[:, 7:8],
                OP.subtract, OP.mult), sV)
            b.wait("tensor", sV)
            for kg in range(2):
                b.wait("tensor", sS)
                for k8 in range(8):
                    k = kg * 8 + k8
                    b.on("tensor", lambda e, k=k, k8=k8, kg=kg: e.transpose(
                        pT.rearrange("p (g k c) -> p g k c", g=2, k=8)[:, kg, k8, :],
                        xnt[:, k * 128:(k + 1) * 128], identb[:]),
                        sT if k8 == 7 else None)
                b.wait("scalar", sT)
                b.on("scalar", lambda e, kg=kg, i=i: e.copy(
                    xns[:, kg * 8:(kg + 1) * 8, i * 128:(i + 1) * 128],
                    pT.rearrange("p (g k c) -> p g k c", g=2, k=8)[:, kg, :, :]), sS)
        BAR()
        if "xn" in dump_t:
            DUMP("xn", dump_t["xn"][blk], xns[:])

        w1_ready = {}
        cw_ready = {}
        w1_ready[0] = b.dma("gpsimd", w1sb[:, 0, :],
                            w1c[0].rearrange("p a b -> p (a b)"), sLw)
        cw_ready[0] = b.dma("gpsimd", cwg[:, 0, :],
                            cwdiag_i[0].rearrange("p a b -> p (a b)"), sLw)
        w1_ready[1] = b.dma("gpsimd", w1sb[:, 1, :],
                            w1c[1].rearrange("p a b -> p (a b)"), sLw)
        cw_ready[1] = b.dma("gpsimd", cwg[:, 1, :],
                            cwdiag_i[1].rearrange("p a b -> p (a b)"), sLw)
        for g in range(NGA1):
            rb = g % 2
            b.wait("tensor", sLw, w1_ready[g])
            b.wait("tensor", sS)
            for k in range(16):
                b.on("tensor", lambda e, g=g, k=k: e.matmul(
                    pA[:, 0, :], w1sb[:, g % 2, k * 128:(k + 1) * 128],
                    xns[:, k, :], start=(k == 0), stop=(k == 15)),
                    sT if k == 15 else None)
            if g + 2 < NGA1:
                b.wait("gpsimd", sT)
                w1_ready[g + 2] = b.dma(
                    "gpsimd", w1sb[:, g % 2, :],
                    w1c[g + 2].rearrange("p a b -> p (a b)"), sLw)
                if g + 2 < NGX:
                    cw_ready[g + 2] = b.dma(
                        "gpsimd", cwg[:, g % 2, :],
                        cwdiag_i[g + 2].rearrange("p a b -> p (a b)"), sLw)
            b.wait("scalar", sT)
            b.wait("scalar", sV)
            if g < NGX:
                b.on("scalar", lambda e, g=g, rb=rb: e.copy(
                    rawr[:, rb, 3:3 + TB], pA[:, 0, :]), sS)
                b.wait("vector", sS)
                b.on("vector", lambda e, g=g, rb=rb: e.tensor_copy(
                    rawr[:, rb, 0:3], tails[:, g, :]), sV)
                b.wait("tensor", sV)
                b.wait("tensor", sLw, cw_ready[g])
                for j in range(4):
                    b.on("tensor", lambda e, g=g, j=j, rb=rb: e.matmul(
                        pA[:, 1, :], cwg[:, g % 2, j * 128:(j + 1) * 128],
                        rawr[:, rb, j:TB + j],
                        start=(j == 0), stop=(j == 3)), sT if j == 3 else None)
                b.wait("scalar", sT)
                b.on("scalar", lambda e, g=g: e.activation(
                    cvo[:, g, :], pA[:, 1, :], AF.Silu,
                    bias=convb[:, g:g + 1], scale=1.0), sS)
                b.on("vector", lambda e, g=g, rb=rb: e.tensor_copy(
                    tails[:, g, :], rawr[:, rb, TB:TB + 3]), sV)
            else:
                b.on("scalar", lambda e: e.copy(
                    dtraw[:], pA[0:CH, 0, :]), sS)
        BAR()
        if "dtraw" in dump_t:
            DUMP("dtraw", dump_t["dtraw"][blk], dtraw[:])
        if "cvo" in dump_t:
            DUMP("cvo", dump_t["cvo"][blk], cvo[:])

        if ssd_stage < 1:
            continue
        b.wait("scalar", sS)
        b.on("scalar", lambda e: e.activation(
            dtsp[:], dtraw[:], AF.Sigmoid, bias=dtbn[:], scale=-1.0))
        b.on("scalar", lambda e: e.activation(dtsp[:], dtsp[:], AF.Ln), sS)
        b.wait("vector", sS)
        b.on("vector", lambda e: e.tensor_scalar(
            adisc[:], dtsp[:], ealog[:], None, OP.mult))
        b.on("vector", lambda e: e.tensor_scalar(
            dtsp[:], dtsp[:], -1.0, None, OP.mult), sV)
        b.on("vector", lambda e: e.tensor_tensor_scan(
            cumA[:], mres512[:], adisc[:], zcol[:], OP.mult, OP.add), sV)
        b.wait("scalar", sV)
        b.on("scalar", lambda e: e.copy(
            ends[:].unsqueeze(2),
            cumA[:].rearrange("h (c l) -> h c l", c=NCHB)[:, :, CH - 1:CH]), sS)
        b.wait("vector", sS)
        b.on("vector", lambda e: e.tensor_tensor(
            scr1[0:CH, 0:TB].rearrange("h (c l) -> h c l", c=NCHB),
            ends[:].unsqueeze(2).broadcast_to((CH, NCHB, CH)),
            cumA[:].rearrange("h (c l) -> h c l", c=NCHB), OP.subtract), sV)
        b.wait("scalar", sV)
        b.on("scalar", lambda e: e.activation(decay[:], scr1[0:CH, 0:TB], AF.Exp))
        b.on("scalar", lambda e: e.activation(sdo[:], cumA[:], AF.Exp), sS)

        if ssd_stage < 2:
            continue
        b.wait("tensor", sS)
        b.wait("tensor", sV)
        for ci in range(NCHB):
            b.on("tensor", lambda e, ci=ci: e.transpose(
                pW[0:CH, ci * CH:(ci + 1) * CH],
                cumA[:, ci * CH:(ci + 1) * CH], identf[0:CH, 0:CH]),
                sT if ci == NCHB - 1 else None)
        b.wait("scalar", sT)
        b.on("scalar", lambda e: e.copy(
            cumATs[0:CH].rearrange("p c l -> p (c l)"), pW[0:CH]), sS)
        if ssd_stage < 2.2:
            continue
        b.wait("scalar", sS)
        cat_rdy = b.dma("scalar", cumATs[CH:128].rearrange("p c l -> p (c l)"),
                        cumATs[0:CH].rearrange("p c l -> p (c l)"), sLs)
        b.wait("vector", sLs, cat_rdy)
        if ssd_stage < 2.3:
            continue
        b.wait("vector", sS)
        b.on("vector", lambda e: e.tensor_copy(
            vall[0:CH, :, :],
            cumATs.rearrange("p c (hp two) -> p c hp two", two=2)[0:CH, :, :, 0]))
        b.on("vector", lambda e: e.tensor_copy(
            vall[CH:128, :, :],
            cumATs.rearrange("p c (hp two) -> p c hp two", two=2)[CH:128, :, :, 1]),
            sV)
        if ssd_stage < 2.4:
            continue
        b.wait("tensor", sS)
        for ci in range(NCHB):
            b.on("tensor", lambda e, ci=ci: e.transpose(
                pW[0:CH, ci * CH:(ci + 1) * CH],
                sdo[:, ci * CH:(ci + 1) * CH], identf[0:CH, 0:CH]),
                sT if ci == NCHB - 1 else None)
        b.wait("scalar", sT)
        b.on("scalar", lambda e: e.copy(
            sdoTc[:].rearrange("p c l -> p (c l)"), pW[0:CH, :]), sS)
        if ssd_stage < 2.5:
            continue
        b.wait("tensor", sS)
        for pi in range(NCHB // 2):
            b.on("tensor", lambda e, pi=pi: e.transpose(
                pW[:, pi * CH:(pi + 1) * CH],
                dtsp[:, pi * 128:(pi + 1) * 128], identf[0:CH, 0:CH]),
                sT if pi == 3 else None)
        b.wait("scalar", sT)
        b.on("scalar", lambda e: e.copy(
            dtT[:].rearrange("p c l -> p (c l)"), pW[:, 0:256]), sS)
        b.wait("tensor", sS)
        for pi in range(NCHB // 2):
            b.on("tensor", lambda e, pi=pi: e.transpose(
                pW[:, pi * CH:(pi + 1) * CH],
                decay[:, pi * 128:(pi + 1) * 128], identf[0:CH, 0:CH]),
                sT if pi == 3 else None)
        b.wait("scalar", sT)
        b.on("scalar", lambda e: e.copy(
            decT[:].rearrange("p c l -> p (c l)"), pW[:, 0:256]), sS)
        if ssd_stage < 2.6:
            continue
        b.wait("tensor", sS)
        for ci in range(NCHB):
            b.on("tensor", lambda e, ci=ci: e.transpose(
                pW[0:1, ci * CH:ci * CH + CH],
                ends[:, ci:ci + 1], identf[0:CH, 0:CH]),
                sT if ci == NCHB - 1 else None)
        b.wait("scalar", sT)
        b.on("scalar", lambda e: e.copy(erow8[:], pW[0:1, :]), sS)
        b.wait("tensor", sS)
        b.on("tensor", lambda e: e.matmul(
            pW[:], ones1[:], erow8[:], start=True, stop=True), sT)
        b.wait("scalar", sT)
        b.on("scalar", lambda e: e.activation(
            lamrep[:].rearrange("p c l -> p (c l)"), pW[:], AF.Exp), sS)

        if ssd_stage < 3:
            continue
        for wv in range(2):
            for cw4 in range(4):
                ci = wv * 4 + cw4
                bb = ci % 2
                b.wait("vector", sT)
                b.on("vector", lambda e, ci=ci, bb=bb: e.tensor_copy(
                    b2[:, bb, 0:CH], cvo[:, 32, ci * CH:(ci + 1) * CH]))
                b.on("vector", lambda e, ci=ci, bb=bb: e.tensor_copy(
                    b2[:, bb, CH:128], cvo[:, 32, ci * CH:(ci + 1) * CH]), sV)
                b.wait("tensor", sV)
                b.on("tensor", lambda e, cw4=cw4, ci=ci, bb=bb: e.matmul(
                    pSc[:, cw4 * CH:(cw4 + 1) * CH], b2[:, bb, :],
                    cvo[:, 33, ci * CH:(ci + 1) * CH], start=True, stop=True), sT)

            b.wait("tensor", sS)
            for hp in range(32):
                b.wait("tensor", sV)
                b.on("tensor", lambda e, hp=hp, wv=wv: e.matmul(
                    pW[:, 0:256], selp[:, hp, :],
                    cumA[:, wv * 256:(wv + 1) * 256], start=True, stop=True), sT)
                b.wait("vector", sT)
                b.on("vector", lambda e, hp=hp, wv=wv: e.tensor_tensor(
                    wtmp[:].rearrange("p (c l) -> p c l", c=4),
                    pW[:, 0:256].rearrange("p (c l) -> p c l", c=4),
                    vall[:, wv * 4:(wv + 1) * 4, hp:hp + 1]
                    .broadcast_to((128, 4, CH)), OP.subtract), sV)
                b.on("vector", lambda e: e.tensor_tensor(
                    wtmp[:].rearrange("p (c l) -> p c l", c=4),
                    wtmp[:].rearrange("p (c l) -> p c l", c=4),
                    mtri[:].unsqueeze(1).broadcast_to((128, 4, CH)), OP.add), sV)
                b.wait("scalar", sV)
                b.on("scalar", lambda e: e.activation(wexp[:], wtmp[:], AF.Exp), sS)
                b.wait("vector", sS)
                b.on("vector", lambda e, hp=hp: e.tensor_tensor(
                    wt[:, hp, :], wexp[:], pSc[:, 0:256], OP.mult), sV)

            if ssd_stage < 4:
                continue
            if blk == 0 and wv == 0 and "wt0" in dump_t:
                DUMP("wt0", dump_t["wt0"], wt[:])
                DUMP("vall0", dump_t["vall0"], vall[:])
            for cw4 in range(4):
                ci = wv * 4 + cw4
                c_glob = blk * NCHB + ci
                pi = ci // 2
                if ci % 2 == 0:
                    for rnd in range(2):
                        b.wait("tensor", sV)
                        b.wait("tensor", sS)
                        b.wait("vector", sLs)
                        b.wait("vector", sT)
                        for gj in range(16):
                            g = rnd * 16 + gj
                            b.on("tensor", lambda e, g=g, gj=gj, pi=pi: e.transpose(
                                pT.rearrange("p (g c) -> p g c", g=16)[:, gj, :],
                                cvo[:, g, pi * 128:(pi + 1) * 128], identb[:]),
                                sT if gj == 15 else None)
                        b.wait("vector", sT)
                        b.on("vector", lambda e, rnd=rnd, pi=pi: e.tensor_tensor(
                            xdt[:, rnd * 2048:(rnd + 1) * 2048].rearrange(
                                "p (h q) -> p h q", q=CH),
                            pT[:].rearrange("p (h q) -> p h q", q=CH),
                            dtT[:, pi, rnd * 32:(rnd + 1) * 32].unsqueeze(2)
                            .broadcast_to((128, 32, CH)), OP.mult), sV)
                        b.on("vector", lambda e, rnd=rnd: e.tensor_tensor(
                            xdv[:, rnd * 2048:(rnd + 1) * 2048].rearrange(
                                "p (h q) -> p h q", q=CH),
                            pT[:].rearrange("p (h q) -> p h q", q=CH),
                            drep[:, rnd * 32:(rnd + 1) * 32].unsqueeze(2)
                            .broadcast_to((128, 32, CH)), OP.mult), sV)
                    b.on("vector", lambda e, pi=pi: e.tensor_tensor(
                        xdd[:].rearrange("p (h q) -> p h q", q=CH),
                        xdt[:].rearrange("p (h q) -> p h q", q=CH),
                        decT[:, pi, :].unsqueeze(2).broadcast_to((128, NH, CH)),
                        OP.mult), sV)
                b.wait("scalar", sV)
                b.wait("scalar", sT)
                b.dma("scalar", arT[0:CH, :],
                      xdt[(ci % 2) * CH:(ci % 2) * CH + CH, :], sLs)
                ar_rdy = b.dma("scalar", arB[CH:128, :],
                               xdt[(ci % 2) * CH:(ci % 2) * CH + CH, :], sLs)
                bt_i = ci % 4
                b.wait("tensor", sS)
                b.wait("tensor", sV)
                if ci % 2 == 0:
                    b.on("tensor", lambda e, ci=ci: e.transpose(
                        pT[0:CH, 0:128], cvo[:, 32, ci * CH:(ci + 1) * CH],
                        identb[:]), sT)
                    b.wait("scalar", sT)
                    b.on("scalar", lambda e, bt_i=bt_i: e.copy(
                        btok[0:CH, bt_i, :], pT[0:CH, 0:128]), sS)
                else:
                    b.on("tensor", lambda e, ci=ci: e.transpose(
                        pT[0:CH, 0:128], cvo[:, 32, ci * CH:(ci + 1) * CH],
                        identb[:]), sT)
                    b.wait("scalar", sT)
                    b.on("scalar", lambda e: e.copy(
                        btstage[:], pT[0:CH, 0:128]), sS)
                    b.wait("scalar", sS)
                    bt_rdy = b.dma("scalar", btok[CH:128, bt_i, :], btstage[:], sLs)
                    b.wait("tensor", sLs, bt_rdy)

                if ssd_stage < 5:
                    continue
                b.wait("tensor", sLs, ar_rdy)
                b.wait("tensor", sS)
                b.wait("vector", sLx)
                for sl in range(8):
                    b.on("tensor", lambda e, sl=sl, bt_i=bt_i: e.matmul(
                        pSc[:], btok[:, bt_i, :], xdd[:, sl * 512:(sl + 1) * 512],
                        start=True, stop=True), sT)
                    b.wait("vector", sT)
                    b.on("vector", lambda e, ci=ci, sl=sl: e.tensor_tensor(
                        srt[:].rearrange("p (h q) -> p h q", q=CH),
                        srun[:, sl * 512:(sl + 1) * 512].rearrange(
                            "p (h q) -> p h q", q=CH),
                        lamrep[:, ci, sl * 8:(sl + 1) * 8].unsqueeze(2)
                        .broadcast_to((128, 8, CH)), OP.mult), sV)
                    b.on("vector", lambda e, sl=sl: e.tensor_tensor(
                        srun[:, sl * 512:(sl + 1) * 512], srt[:], pSc[:], OP.add),
                        sV)
                    b.wait("tensor", sV)
                    b.on("tensor", lambda e, ci=ci, sl=sl: e.matmul(
                        pY[0:CH, :], cvo[:, 33, ci * CH:(ci + 1) * CH],
                        srun[:, sl * 512:(sl + 1) * 512],
                        start=True, stop=True), sT)
                    b.on("tensor", lambda e, ci=ci, sl=sl: e.matmul(
                        pY2[0:CH, :], i64x[:, (ci % 2) * CH:(ci % 2) * CH + CH],
                        xdv[:, sl * 512:(sl + 1) * 512],
                        start=True, stop=False))
                    for j in range(8):
                        h = sl * 8 + j
                        src_ar = arT if h % 2 == 0 else arB
                        b.on("tensor", lambda e, h=h, j=j, cw4=cw4, src_ar=src_ar:
                             e.matmul(
                                 pY2[0:CH, j * CH:(j + 1) * CH],
                                 wt[:, h // 2, cw4 * CH:(cw4 + 1) * CH],
                                 src_ar[:, h * CH:(h + 1) * CH],
                                 start=False, stop=(j == 7)), sT if j == 7 else None)
                    b.wait("vector", sT)
                    b.on("vector", lambda e, ci=ci, sl=sl: e.tensor_tensor(
                        yacc[:, sl * 512:(sl + 1) * 512].rearrange(
                            "p (h q) -> p h q", q=CH),
                        pY[0:CH, :].rearrange("p (h q) -> p h q", q=CH),
                        sdoTc[0:CH, ci, sl * 8:(sl + 1) * 8].unsqueeze(2)
                        .broadcast_to((CH, 8, CH)), OP.mult), sV)
                    b.on("vector", lambda e, sl=sl: e.tensor_tensor(
                        ytile[:, sl * 512:(sl + 1) * 512],
                        yacc[:, sl * 512:(sl + 1) * 512], pY2[0:CH, :], OP.add), sV)
                    b.wait("tensor", sV)
                b.wait("sync", sV)
                b.dma("sync", ysc[c_glob * CH:(c_glob + 1) * CH, :],
                      ytile[:], sLx)
                if blk == 0 and ci == 0 and "sr0" in dump_t:
                    DUMP("sr0", dump_t["sr0"], srun[:])
                    DUMP("ar0", dump_t["ar0"][0], arT[:])
                    DUMP("ar0", dump_t["ar0"][1], arB[:])
                    DUMP("xd0", dump_t["xd0"][0], xdd[:])
                    DUMP("xd0", dump_t["xd0"][1], xdv[:])
                    b.wait("scalar", sLw)
                    b.wait("vector", sLw)
        BAR()
        if "cumA" in dump_t:
            DUMP("cumA", dump_t["cumA"][blk], cumA[:])
        if "dtsp" in dump_t:
            DUMP("dtsp", dump_t["dtsp"][blk], dtsp[:])
        if "srun" in dump_t:
            DUMP("srun", dump_t["srun"][blk], srun[:])
        if "lam" in dump_t:
            DUMP("lam", dump_t["lam"][blk], lamrep[:])

    if "y" in dump_t:
        b.wait("sync", sLx)
        b.dma("sync", dump_t["y"], ysc, sLx)
        b.wait("sync", sLx)

    a5 = Arena()
    wz = a5.take([128, 16, 4096], BF16)
    ytm = a5.take([128, DIN], BF16)
    sz = a5.take([128, DIN], BF16)
    ygs = a5.take([128, DIN], BF16)
    xnf = a5.take([128, 16, 128], BF16)
    for k in range(16):
        b.dma("gpsimd", wz[:, k, :], wzt[k], sLw)
    for t in range(16):
        b.wait("sync", sV)
        yrdy = b.dma("sync", ytm[:], ysc[t * 128:(t + 1) * 128, :], sLx)
        xrdy = b.dma("sync", xtm[:], x_in[t * 128:(t + 1) * 128, :], sLx)
        b.wait("scalar", sLx, xrdy)
        b.wait("vector", sLx, xrdy)
        b.on("scalar", lambda e: e.activation(scr1[:], xtm[:, 0:1024], AF.Square), sS)
        b.on("vector", lambda e: e.reduce_sum(stat[:, 0:1], xtm[:], axis=AX.X))
        b.wait("vector", sS)
        b.on("vector", lambda e: e.reduce_sum(stat[:, 1:2], scr1[:], axis=AX.X), sV)
        b.wait("scalar", sV)
        b.on("scalar", lambda e: e.activation(scr1[:], xtm[:, 1024:2048], AF.Square), sS)
        b.wait("vector", sS)
        b.on("vector", lambda e: e.reduce_sum(stat[:, 6:7], scr1[:], axis=AX.X))
        b.on("vector", lambda e: e.tensor_scalar(
            stat[:, 2:3], stat[:, 0:1], 1.0 / DM, None, OP.mult))
        b.on("vector", lambda e: e.tensor_tensor(
            stat[:, 1:2], stat[:, 1:2], stat[:, 6:7], OP.add))
        b.on("vector", lambda e: e.tensor_tensor(
            stat[:, 4:5], stat[:, 2:3], stat[:, 2:3], OP.mult))
        b.on("vector", lambda e: e.tensor_scalar(
            stat[:, 3:4], stat[:, 1:2], 1.0 / DM, None, OP.mult))
        b.on("vector", lambda e: e.tensor_copy(scr1[:, 40:41], scr1[:, 41:42]))
        b.on("vector", lambda e: e.tensor_scalar(
            stat[:, 5:6], stat[:, 3:4], stat[:, 4:5], 1e-5,
            OP.subtract, OP.add), sV)
        b.wait("scalar", sV)
        b.on("scalar", lambda e: e.activation(stat[:, 6:7], stat[:, 5:6], AF.Sqrt), sS)
        b.wait("vector", sS)
        b.on("vector", lambda e: e.reciprocal(stat[:, 7:8], stat[:, 6:7]))
        b.wait("vector", sT)
        b.on("vector", lambda e: e.tensor_scalar(
            xnt[:], xtm[:], stat[:, 2:3], stat[:, 7:8], OP.subtract, OP.mult), sV)
        b.wait("tensor", sV)
        b.wait("tensor", sS)
        for k in range(16):
            b.on("tensor", lambda e, k=k: e.transpose(
                pT.rearrange("p (k c) -> p k c", k=16)[:, k, :],
                xnt[:, k * 128:(k + 1) * 128], identb[:]), sT if k == 15 else None)
        b.wait("scalar", sT)
        b.on("scalar", lambda e: e.copy(
            xnf[:], pT.rearrange("p (k c) -> p k c", k=16)), sS)
        b.wait("tensor", sS)
        b.wait("tensor", sLw)
        for sl in range(8):
            bk = sl % 2
            b.wait("tensor", sS)
            for k in range(16):
                b.on("tensor", lambda e, sl=sl, k=k, bk=bk: e.matmul(
                    pA[:, bk, :], xnf[:, k, :], wz[:, k, sl * 512:(sl + 1) * 512],
                    start=(k == 0), stop=(k == 15)), sT if k == 15 else None)
            b.wait("scalar", sT)
            b.on("scalar", lambda e, sl=sl, bk=bk: e.activation(
                sz[:, sl * 512:(sl + 1) * 512], pA[:, bk, :], AF.Silu), sS)
        b.wait("vector", sS)
        b.wait("vector", sLx, yrdy)
        b.on("vector", lambda e: e.tensor_tensor(ygs[:], ytm[:], sz[:], OP.mult), sV)
        if t == 0 and "sz0" in dump_t:
            DUMP("sz0", dump_t["sz0"], sz[:])
            DUMP("yg0", dump_t["yg0"], ygs[:])
            b.wait("scalar", sLw)
            b.wait("vector", sLw)
        for q in range(4):
            b.wait("scalar", sV)
            b.on("scalar", lambda e, q=q: e.activation(
                scr1[:], ygs[:, q * 1024:(q + 1) * 1024], AF.Square), sS)
            b.wait("vector", sS)
            b.on("vector", lambda e, q=q: e.reduce_sum(
                stat[:, q:q + 1], scr1[:], axis=AX.X), sV)
        b.on("vector", lambda e: e.tensor_tensor(
            stat[:, 4:5], stat[:, 0:1], stat[:, 1:2], OP.add))
        b.on("vector", lambda e: e.tensor_tensor(
            stat[:, 5:6], stat[:, 2:3], stat[:, 3:4], OP.add))
        b.on("vector", lambda e: e.tensor_copy(scr1[:, 40:41], scr1[:, 41:42]))
        b.on("vector", lambda e: e.tensor_tensor(
            stat[:, 6:7], stat[:, 4:5], stat[:, 5:6], OP.add))
        b.on("vector", lambda e: e.tensor_copy(scr1[:, 42:43], scr1[:, 43:44]))
        b.on("vector", lambda e: e.tensor_scalar(
            stat[:, 3:4], stat[:, 6:7], 1.0 / DIN, 1e-5, OP.mult, OP.add), sV)
        b.wait("scalar", sV)
        b.on("scalar", lambda e: e.activation(stat[:, 4:5], stat[:, 3:4], AF.Sqrt), sS)
        b.wait("vector", sS)
        b.on("vector", lambda e: e.reciprocal(stat[:, 5:6], stat[:, 4:5]))
        b.on("vector", lambda e: e.tensor_copy(stat[:, 7:8], stat[:, 5:6]))
        b.on("vector", lambda e: e.tensor_scalar(
            ygs[:], ygs[:], stat[:, 5:6], None, OP.mult), sV)
        b.wait("sync", sV)
        b.dma("sync", ynsc[t * 128:(t + 1) * 128, :], ygs[:], sLx)
    BAR()

    a5b = Arena()
    wo = a5b.take([128, 32, 2048], BF16)
    ynf = a5b.take([128, 32, 128], BF16)
    yntm = a5b.take([128, DIN], BF16)
    x1t = a5b.take([128, DM], F32)
    xn2f = a5b.take([128, DM], F32)
    xnf32 = yntm.bitcast(F32).rearrange("p (a c) -> p a c", a=16)
    wrtf = a5b.take([128, 16, 8], F32)
    lgf2 = a5b.take([128, 64], F32)
    wetile2 = a5b.take([128, 8], BF16)
    westg = a5b.take([8, 128], BF16)
    for k in range(32):
        b.dma("gpsimd", wo[:, k, :], wot[k], sLw)
    for k in range(16):
        b.dma("sync", wrtf[:, k, :], wrt[k], sLx)
    b.wait("vector", sLx)
    b.on("vector", lambda e: e.memset(lgf2[:, 8:64], -1e30), sV)
    b.wait("vector", sLw)
    for k in range(32):
        b.on("vector", lambda e, k=k: e.tensor_scalar(
            wo[:, k, :], wo[:, k, :], nw[:, k:k + 1], None, OP.mult),
            sV if k == 31 else None)
    for t in range(16):
        b.wait("sync", sV)
        b.wait("sync", sLs)
        b.wait("scalar", sLs)
        yrdy = b.dma("sync", yntm[:], ynsc[t * 128:(t + 1) * 128, :], sLx)
        xrdy = b.dma("sync", xtm[:], x_in[t * 128:(t + 1) * 128, :], sLx)
        b.wait("tensor", sLx, yrdy)
        b.wait("tensor", sV)
        b.wait("tensor", sS)
        for rnd in range(2):
            for k8 in range(16):
                k = rnd * 16 + k8
                b.on("tensor", lambda e, k=k, k8=k8: e.transpose(
                    pT.rearrange("p (k c) -> p k c", k=16)[:, k8, :],
                    yntm[:, k * 128:(k + 1) * 128], identb[:]),
                    sT if k8 == 15 else None)
            b.wait("scalar", sT)
            b.on("scalar", lambda e, rnd=rnd: e.copy(
                ynf[:, rnd * 16:(rnd + 1) * 16, :],
                pT.rearrange("p (k c) -> p k c", k=16)), sS)
            b.wait("tensor", sS)
        b.wait("vector", sLx, xrdy)
        for sl in range(4):
            bk = sl % 2
            b.wait("tensor", sV)
            for k in range(32):
                b.on("tensor", lambda e, sl=sl, k=k, bk=bk: e.matmul(
                    pA[:, bk, :], ynf[:, k, :], wo[:, k, sl * 512:(sl + 1) * 512],
                    start=(k == 0), stop=(k == 31)), sT if k == 31 else None)
            b.wait("vector", sT)
            b.on("vector", lambda e, sl=sl, bk=bk: e.tensor_tensor(
                x1t[:, sl * 512:(sl + 1) * 512], pA[:, bk, :],
                xtm[:, sl * 512:(sl + 1) * 512], OP.add), sV)
        b.wait("gpsimd", sV)
        b.dma("gpsimd", x1sc[t * 128:(t + 1) * 128, :], x1t[:], sLw)
        b.wait("scalar", sV)
        b.on("scalar", lambda e: e.activation(scr1[:], x1t[:, 0:1024], AF.Square), sS)
        b.on("vector", lambda e: e.reduce_sum(stat[:, 0:1], x1t[:], axis=AX.X))
        b.wait("vector", sS)
        b.on("vector", lambda e: e.reduce_sum(stat[:, 1:2], scr1[:], axis=AX.X), sV)
        b.wait("scalar", sV)
        b.on("scalar", lambda e: e.activation(scr1[:], x1t[:, 1024:2048], AF.Square), sS)
        b.wait("vector", sS)
        b.on("vector", lambda e: e.reduce_sum(stat[:, 6:7], scr1[:], axis=AX.X))
        b.on("vector", lambda e: e.tensor_scalar(
            stat[:, 2:3], stat[:, 0:1], 1.0 / DM, None, OP.mult))
        b.on("vector", lambda e: e.tensor_tensor(
            stat[:, 1:2], stat[:, 1:2], stat[:, 6:7], OP.add))
        b.on("vector", lambda e: e.tensor_tensor(
            stat[:, 4:5], stat[:, 2:3], stat[:, 2:3], OP.mult))
        b.on("vector", lambda e: e.tensor_scalar(
            stat[:, 3:4], stat[:, 1:2], 1.0 / DM, None, OP.mult))
        b.on("vector", lambda e: e.tensor_copy(scr1[:, 40:41], scr1[:, 41:42]))
        b.on("vector", lambda e: e.tensor_scalar(
            stat[:, 5:6], stat[:, 3:4], stat[:, 4:5], 1e-5,
            OP.subtract, OP.add), sV)
        b.wait("scalar", sV)
        b.on("scalar", lambda e: e.activation(stat[:, 6:7], stat[:, 5:6], AF.Sqrt), sS)
        b.wait("vector", sS)
        b.on("vector", lambda e: e.reciprocal(stat[:, 7:8], stat[:, 6:7]))
        b.wait("vector", sT)
        b.on("vector", lambda e: e.tensor_scalar(
            xnt[:], x1t[:], stat[:, 2:3], stat[:, 7:8], OP.subtract, OP.mult))
        b.on("vector", lambda e: e.tensor_scalar(
            xn2f[:], x1t[:], stat[:, 2:3], stat[:, 7:8], OP.subtract, OP.mult), sV)
        b.wait("tensor", sV)
        b.wait("tensor", sS)
        for k in range(16):
            b.on("tensor", lambda e, k=k: e.transpose(
                pT.rearrange("p (k c) -> p k c", k=16)[:, k, :],
                xnt[:, k * 128:(k + 1) * 128], identb[:]), sT if k == 15 else None)
        b.wait("scalar", sT)
        b.on("scalar", lambda e: e.copy(
            ynf[:, 0:16, :], pT.rearrange("p (k c) -> p k c", k=16)), sS)
        b.wait("scalar", sS)
        b.dma("scalar", xfsc[t], ynf[:, 0:16, :], sLs)
        for rnd in range(4):
            b.wait("tensor", sS)
            for k4 in range(4):
                k = rnd * 4 + k4
                b.on("tensor", lambda e, k=k, k4=k4: e.transpose(
                    pW.rearrange("p (a c) -> p a c", a=4)[:, k4, :],
                    xn2f[:, k * 128:(k + 1) * 128], identf[:]),
                    sT if k4 == 3 else None)
            b.wait("scalar", sT)
            b.on("scalar", lambda e, rnd=rnd: e.copy(
                xnf32[:, rnd * 4:(rnd + 1) * 4, :],
                pW.rearrange("p (a c) -> p a c", a=4)), sS)
        b.wait("tensor", sS)
        for k in range(16):
            b.on("tensor", lambda e, k=k: e.matmul(
                pSc[:, 0:8], xnf32[:, k, :], wrtf[:, k, :],
                start=(k == 0), stop=(k == 15)), sT if k == 15 else None)
        b.wait("scalar", sT)
        b.on("scalar", lambda e: e.copy(lgf2[:, 0:8], pSc[:, 0:8]), sS)
        b.wait("vector", sS)
        DR = lambda e: e.drain()
        b.on("vector", lambda e: e.reduce_max(stat[:, 0:1], lgf2[:], axis=AX.X))
        b.on("vector", DR)
        b.on("vector", lambda e: e.tensor_scalar(
            scr1[:, 64:128], lgf2[:], stat[:, 0:1], None, OP.is_equal))
        b.on("vector", DR)
        b.on("vector", lambda e: e.scalar_tensor_tensor(
            scr1[:, 128:192], scr1[:, 64:128], -1e30, lgf2[:], OP.mult, OP.add))
        b.on("vector", DR)
        b.on("vector", lambda e: e.tensor_scalar(
            stat[:, 1:2], stat[:, 0:1], -1.0, None, OP.mult))
        b.on("vector", DR)
        b.on("vector", lambda e: e.reduce_max(stat[:, 2:3], scr1[:, 128:192],
                                              axis=AX.X), sV)
        b.wait("scalar", sV)
        b.on("scalar", lambda e: e.activation(
            scr1[:, 192:256], lgf2[:], AF.Exp, bias=stat[:, 1:2], scale=1.0), sS)
        b.wait("vector", sS)
        b.on("vector", lambda e: e.reduce_sum(stat[:, 3:4], scr1[:, 192:256],
                                              axis=AX.X))
        b.on("vector", DR)
        b.on("vector", lambda e: e.tensor_scalar(
            scr1[:, 256:320], lgf2[:], stat[:, 2:3], None, OP.is_ge))
        b.on("vector", DR)
        b.on("vector", lambda e: e.tensor_tensor(
            scr1[:, 320:384], scr1[:, 192:256], scr1[:, 256:320], OP.mult))
        b.on("vector", DR)
        b.on("vector", lambda e: e.reciprocal(stat[:, 4:5], stat[:, 3:4]))
        b.on("vector", DR)
        b.on("vector", lambda e: e.tensor_scalar(
            scr1[:, 384:448], scr1[:, 320:384], stat[:, 4:5], None, OP.mult))
        b.on("vector", DR)
        b.on("vector", lambda e: e.tensor_copy(wetile2[:], scr1[:, 384:392]), sV)
        b.wait("tensor", sV)
        b.wait("tensor", sS)
        b.on("tensor", lambda e: e.transpose(
            pT[0:8, 0:128], wetile2[:], identb[:]), sT)
        b.wait("scalar", sT)
        b.on("scalar", lambda e: e.copy(westg[:], pT[0:8, 0:128]), sS)
        b.wait("scalar", sS)
        b.dma("scalar", wesc[:, t * 128:(t + 1) * 128], westg[:], sLs)
    BAR()

    HT = 512
    for hf in range(4):
        fa = Arena()
        h_base = fa.take([128, NJ, HT], BF16)
        bdt_all = fa.take([128, DM], BF16)
        t_ewP = fa.take([128, HT], BF16)
        wd_mark = fa.off
        xfh = fa.take([128, 4, 16, 128], BF16)
        bgt_all = fa.take([128, DFF], BF16)
        but_all = fa.take([128, DFF], BF16)
        adt_all = fa.take([128, NE, NJ * RK], BF16)
        pmg_all = fa.take([128, NE, HT], BF16)
        pmu_all = fa.take([128, NE, HT], BF16)
        wgsb = fa.take([128, 2, 2048], BF16)
        wusb = fa.take([128, 2, 2048], BF16)
        agt_sb = fa.take([128, 16, 128], BF16)
        aut_sb = fa.take([128, 16, 128], BF16)
        wrt_sb = fa.take([128, 16, 8], BF16)
        weT = fa.take([8, HT], BF16)
        wetile = fa.take([128, 8], BF16)
        gate_sb = fa.take([128, HT], BF16)
        up_sb = fa.take([128, HT], BF16)
        sbase = fa.take([128, HT], BF16)
        sgin = fa.take([128, HT], BF16)
        sutile = fa.take([128, HT], BF16)
        h_e = fa.take([128, HT], BF16)
        weRsb = fa.take([128, HT], BF16)
        lgf = fa.take([128, 8], F32)
        fa_wd = Arena()
        fa_wd.off = wd_mark
        wdcol = fa_wd.take([128, NJ, 512], BF16)
        x1tile = fa_wd.take([128, DM], BF16)
        outsb = fa_wd.take([128, 2, 512], F32)

        b.wait("gpsimd", sT)
        b.wait("gpsimd", sV)
        b.wait("gpsimd", sS)
        b.wait("sync", sLs)
        b.wait("sync", sLw)
        for tt in range(4):
            b.dma("sync", xfh[:, tt, :, :].rearrange("p a b -> p (a b)"),
                  xfsc[hf * 4 + tt], sLx)
        if hf == 0:
            pass
        b.on("vector", lambda e: e.memset(bgt_all[:], 0.0))
        b.on("vector", lambda e: e.memset(but_all[:], 0.0))
        b.on("vector", lambda e: e.memset(bdt_all[:], 0.0))
        b.on("vector", lambda e: e.memset(pmg_all[:], 0.0))
        b.on("vector", lambda e: e.memset(pmu_all[:], 0.0), sV)
        b.wait("gpsimd", sV)
        for e_ in range(NE):
            b.dma("gpsimd", bgt_all[16 * e_:16 * e_ + 16, :], bgt_i[e_], sLw)
            b.dma("gpsimd", but_all[16 * e_:16 * e_ + 16, :], but_i[e_], sLw)
            b.dma("gpsimd", bdt_all[16 * e_:16 * e_ + 16, :], bdt_i[e_], sLw)
            b.dma("gpsimd",
                  adt_all[:, e_, :].rearrange("p (a b) -> p a b", a=NJ),
                  adt_i[e_].transpose([1, 0, 2]), sLw)
        for k in range(16):
            b.dma("gpsimd", agt_sb[:, k, :], agt_i[k], sLw)
            b.dma("gpsimd", aut_sb[:, k, :], aut_i[k], sLw)
            b.dma("gpsimd", wrt_sb[:, k, :], wrt[k], sLw)
        b.wait("tensor", sLx)
        b.wait("tensor", sLw)

        if hf == 0 and "xfh0" in dump_t:
            b.wait("gpsimd", sLx)
            b.dma("gpsimd", dump_t["xfh0"], xfh[:].rearrange("p a b c -> p (a b c)"), sLw)
            b.wait("sync", sLw)
        for asb, pm in ((agt_sb, pmg_all), (aut_sb, pmu_all)):
            b.wait("tensor", sS)
            for k in range(16):
                b.on("tensor", lambda e, k=k, asb=asb: e.matmul(
                    pBig[:, 0, :].rearrange("p (a c) -> p a c", a=4),
                    asb[:, k, :], xfh[:, 0:4, k, :],
                    start=(k == 0), stop=(k == 15)), sT if k == 15 else None)
            b.wait("scalar", sT)
            b.on("scalar", lambda e: e.copy(sgin[:], pBig[:, 0, :]), sS)
            b.wait("vector", sS)
            b.on("vector", lambda e: e.tensor_scalar(
                sgin[:], sgin[:], SCALING, None, OP.mult), sV)
            b.wait("scalar", sV)
            for e_ in range(NE):
                b.dma("scalar", pm[16 * e_:16 * e_ + 16, e_, :],
                      sgin[16 * e_:16 * e_ + 16, :], sLs)
            b.wait("tensor", sLs)
            b.wait("vector", sLs)

        b.wait("sync", sS)
        b.dma("sync", weT[:], wesc[:, hf * HT:(hf + 1) * HT], sLx)
        b.wait("tensor", sLx)

        if hf == 0 and "we0" in dump_t:
            DUMP("we0", dump_t["we0"], weT[:])
            b.wait("scalar", sLw)
        wg_ready = {}
        wg_ready[0] = b.dma("gpsimd", wgsb[:, 0, :],
                            wgc[0].rearrange("p a b -> p (a b)"), sLw)
        b.dma("gpsimd", wusb[:, 0, :], wuc[0].rearrange("p a b -> p (a b)"), sLw)
        wg_ready[1] = b.dma("gpsimd", wgsb[:, 1, :],
                            wgc[1].rearrange("p a b -> p (a b)"), sLw)
        b.dma("gpsimd", wusb[:, 1, :], wuc[1].rearrange("p a b -> p (a b)"), sLw)
        for j in range(NJ):
            jb = j % 2
            b.wait("tensor", sLw, wg_ready[j])
            b.wait("tensor", sS)
            b.wait("tensor", sV)
            for k in range(16):
                b.on("tensor", lambda e, k=k, jb=jb: e.matmul(
                    pBig[:, 0, :].rearrange("p (a c) -> p a c", a=4),
                    wgsb[:, jb, k * 128:(k + 1) * 128], xfh[:, 0:4, k, :],
                    start=(k == 0), stop=(k == 15)))
            for k in range(16):
                b.on("tensor", lambda e, k=k, jb=jb: e.matmul(
                    pBig[:, 1, :].rearrange("p (a c) -> p a c", a=4),
                    wusb[:, jb, k * 128:(k + 1) * 128], xfh[:, 0:4, k, :],
                    start=(k == 0), stop=(k == 15)), sT if k == 15 else None)
            if j + 2 < NJ:
                b.wait("gpsimd", sT)
                wg_ready[j + 2] = b.dma(
                    "gpsimd", wgsb[:, j % 2, :],
                    wgc[j + 2].rearrange("p a b -> p (a b)"), sLw)
                b.dma("gpsimd", wusb[:, j % 2, :],
                      wuc[j + 2].rearrange("p a b -> p (a b)"), sLw)
            b.wait("scalar", sT)
            b.wait("scalar", sV)
            b.on("scalar", lambda e: e.copy(gate_sb[:], pBig[:, 0, :]))
            b.on("scalar", lambda e: e.copy(up_sb[:], pBig[:, 1, :]))
            b.on("scalar", lambda e: e.activation(
                sbase[:], pBig[:, 0, :], AF.Silu), sS)
            b.wait("vector", sS)
            b.on("vector", lambda e, j=j: e.tensor_tensor(
                h_base[:, j, :], sbase[:], up_sb[:], OP.mult), sV)
            if hf == 0 and j == 0 and "gate0q" in dump_t:
                DUMP("gate0q", dump_t["gate0q"], gate_sb[:])
                b.wait("scalar", sLw)
            for e_ in range(NE):
                b.wait("tensor", sV)
                b.on("tensor", lambda e, j=j, e_=e_: e.matmul(
                    pBig[:, 2, :], bgt_all[:, j * 128:(j + 1) * 128],
                    pmg_all[:, e_, :], start=True, stop=True))
                b.on("tensor", lambda e, j=j, e_=e_: e.matmul(
                    pBig[:, 3, :], but_all[:, j * 128:(j + 1) * 128],
                    pmu_all[:, e_, :], start=True, stop=True), sT)
                b.wait("vector", sT)
                b.on("vector", lambda e: e.tensor_tensor(
                    sgin[:], gate_sb[:], pBig[:, 2, :], OP.add), sV)
                b.wait("scalar", sV)
                b.on("scalar", lambda e: e.activation(sgin[:], sgin[:], AF.Silu), sS)
                b.on("vector", lambda e: e.tensor_tensor(
                    sutile[:], up_sb[:], pBig[:, 3, :], OP.add), sV)
                b.wait("vector", sS)
                b.on("vector", lambda e: e.tensor_tensor(
                    h_e[:], sgin[:], sutile[:], OP.mult), sV)
                b.wait("tensor", sV)
                for ts4 in range(4):
                    b.on("tensor", lambda e, j=j, e_=e_, ts4=ts4: e.matmul(
                        pBig[:, 4, ts4 * 128 + 16 * e_: ts4 * 128 + 16 * e_ + 16],
                        h_e[:, ts4 * 128:(ts4 + 1) * 128],
                        adt_all[:, e_, j * RK:(j + 1) * RK],
                        start=(j == 0 and e_ == 0 and ts4 == 0),
                        stop=(j == NJ - 1 and ts4 == 3)), sT if ts4 == 3 else None)
        BAR()

        b.wait("scalar", sT)
        b.on("scalar", lambda e: e.copy(sutile[:], pBig[:, 4, :]), sS)
        b.wait("tensor", sS)
        for ts4 in range(4):
            b.on("tensor", lambda e, ts4=ts4: e.transpose(
                pT.rearrange("p (a c) -> p a c", a=16)[:, ts4, :],
                sutile[:, ts4 * 128:(ts4 + 1) * 128], identb[:]),
                sT if ts4 == 3 else None)
        b.on("tensor", lambda e: e.matmul(
            pBig[:, 5, :], sel8[:], weT[:], start=True, stop=True), sT)
        b.wait("scalar", sT)
        b.on("scalar", lambda e: e.copy(
            sgin[:].rearrange("p (a c) -> p a c", a=4),
            pT.rearrange("p (a c) -> p a c", a=16)[:, 0:4, :]), sS)
        b.on("scalar", lambda e: e.copy(weRsb[:], pBig[:, 5, :]), sS)
        b.wait("vector", sS)
        b.on("vector", lambda e: e.scalar_tensor_tensor(
            t_ewP[:], sgin[:], SCALING, weRsb[:], OP.mult, OP.mult), sV)
        if hf == 0 and "tew0" in dump_t:
            DUMP("tew0", dump_t["tew0"], t_ewP[:])
            DUMP("hb0", dump_t["hb0"], h_base[:, 0, :])
            b.wait("scalar", sLw)
            b.wait("vector", sLw)
        BAR()

        for dmq in range(4):
            b.wait("gpsimd", sT)
            wd_rdy = b.dma("gpsimd", wdcol[:].rearrange("p a b -> p (a b)"),
                           wdc[dmq].rearrange("p a b -> p (a b)"), sLw)
            b.wait("tensor", sLw, wd_rdy)
            for tq in range(4):
                t_abs = hf * 4 + tq
                b.wait("sync", sT)
                x1_rdy = b.dma("sync", x1tile[:],
                               x1sc[t_abs * 128:(t_abs + 1) * 128, :], sLx)
                b.wait("tensor", sLx, x1_rdy)
                bk = tq % 2
                b.wait("tensor", sS)
                for j in range(NJ):
                    b.on("tensor", lambda e, j=j, tq=tq, bk=bk: e.matmul(
                        pA[:, bk, :], h_base[:, j, tq * 128:(tq + 1) * 128],
                        wdcol[:, j, :], start=(j == 0), stop=False))
                b.on("tensor", lambda e, tq=tq, bk=bk, dmq=dmq: e.matmul(
                    pA[:, bk, :], t_ewP[:, tq * 128:(tq + 1) * 128],
                    bdt_all[:, dmq * 512:(dmq + 1) * 512],
                    start=False, stop=False))
                b.on("tensor", lambda e, tq=tq, bk=bk, dmq=dmq: e.matmul(
                    pA[:, bk, :], identb[:],
                    x1tile[:, dmq * 512:(dmq + 1) * 512],
                    start=False, stop=True), sT)
                b.wait("scalar", sT)
                b.wait("scalar", sLx)
                b.on("scalar", lambda e, tq=tq, bk=bk: e.copy(
                    outsb[:, tq % 2, :], pA[:, bk, :]), sS)
                b.wait("sync", sS)
                b.dma("sync",
                      out_t[t_abs * 128:(t_abs + 1) * 128,
                            dmq * 512:(dmq + 1) * 512],
                      outsb[:, tq % 2, :], sLx)
        BAR()
    if "x1" in dump_t:
        b.dma("sync", dump_t["x1"], x1sc, sLx)
    if "yn" in dump_t:
        b.dma("sync", dump_t["yn"], ynsc, sLx)
    b.wait("sync", sLx)

    with nc.Block() as block:
        b.finalize(block)
    return nc


def kernel(**inputs):
    shared = _stage_shared(inputs)
    x = np.asarray(inputs["L0_attn_x"], np.float32)
    nc = build()
    in_maps = []
    for c in range(8):
        m = dict(shared)
        m["x"] = np.ascontiguousarray(x[c])
        in_maps.append(m)
    res = run_bass_kernel_spmd(nc, in_maps, core_ids=list(range(8)))
    return np.stack([np.asarray(r["out"], np.float32) for r in res.results])


# version 05 (final)
# speedup vs baseline: 1.0010x; 1.0010x over previous
import numpy as np
import ml_dtypes
import concourse.bass as bass
import concourse.mybir as mybir
from concourse.bass_utils import run_bass_kernel_spmd

F32 = mybir.dt.float32
BF16 = mybir.dt.bfloat16
AF = mybir.ActivationFunctionType
OP = mybir.AluOpType
AX = mybir.AxisListType

L = 2048
DM = 2048
DIN = 4096
NH = 64
HD = 64
DST = 128
DXBC = 4352
NGX = 34
NGA1 = 35
NBLK = 4
TB = 512
CH = 64
NCHB = TB // CH
DFF = 5632
NJ = 44
NE = 8
RK = 16
SCALING = 2.0

BFNP = ml_dtypes.bfloat16


def _stage_shared(ins):
    f32 = np.float32
    s = {}
    w = np.asarray(ins["L0_attn_in_proj_w"], f32)
    w1t = np.ascontiguousarray(w[4096:8512].T)
    w1tp = np.zeros((2048, 4480), f32)
    w1tp[:, :4416] = w1t
    s["w1c"] = np.ascontiguousarray(
        w1tp.reshape(16, 128, 35, 128).transpose(2, 1, 0, 3))
    s["wzt"] = np.ascontiguousarray(w[0:4096].T.reshape(16, 128, 4096))
    wo = np.asarray(ins["L0_attn_out_proj_w"], f32)
    s["wot"] = np.ascontiguousarray(wo.T.reshape(32, 128, 2048))
    cw = np.asarray(ins["L0_attn_conv_w"], f32).reshape(DXBC, 4)
    cwd = np.zeros((34, 4, 128, 128), f32)
    gi = np.arange(128)
    for g in range(34):
        for j in range(4):
            cwd[g, j, gi, gi] = cw[g * 128:(g + 1) * 128, j]
    s["cwdiag"] = np.ascontiguousarray(cwd.transpose(0, 2, 1, 3))
    s["convb"] = np.ascontiguousarray(
        np.asarray(ins["L0_attn_conv_b"], f32).reshape(34, 128).T)
    s["alog"] = np.asarray(ins["L0_attn_A_log"], f32).reshape(NH, 1)
    s["dvec"] = np.asarray(ins["L0_attn_D"], f32).reshape(NH, 1)
    s["dtb"] = np.asarray(ins["L0_attn_dt_bias"], f32).reshape(NH, 1)
    s["nw"] = np.ascontiguousarray(
        np.asarray(ins["L0_attn_norm_w"], f32).reshape(32, 128).T)

    wg = np.asarray(ins["L0_ffn_Wg"], f32)
    wu = np.asarray(ins["L0_ffn_Wu"], f32)
    wd = np.asarray(ins["L0_ffn_Wd"], f32)
    s["wgc"] = np.ascontiguousarray(
        wg.T.reshape(16, 128, 44, 128).transpose(2, 1, 0, 3))
    s["wuc"] = np.ascontiguousarray(
        wu.T.reshape(16, 128, 44, 128).transpose(2, 1, 0, 3))
    s["wdc"] = np.ascontiguousarray(
        wd.T.reshape(44, 128, 4, 512).transpose(2, 1, 0, 3))
    s["wrt"] = np.ascontiguousarray(
        np.asarray(ins["L0_ffn_Wr"], f32).T.reshape(16, 128, 8))
    ag = np.asarray(ins["L0_ffn_Ag"], f32)
    au = np.asarray(ins["L0_ffn_Au"], f32)
    s["agt"] = np.ascontiguousarray(ag.transpose(2, 0, 1).reshape(16, 128, 128))
    s["aut"] = np.ascontiguousarray(au.transpose(2, 0, 1).reshape(16, 128, 128))
    s["bgt"] = np.ascontiguousarray(np.asarray(ins["L0_ffn_Bg"], f32).transpose(0, 2, 1))
    s["but"] = np.ascontiguousarray(np.asarray(ins["L0_ffn_Bu"], f32).transpose(0, 2, 1))
    ad = np.asarray(ins["L0_ffn_Ad"], f32)
    s["adt"] = np.ascontiguousarray(ad.transpose(0, 2, 1).reshape(8, 44, 128, 16))
    s["bdt"] = np.ascontiguousarray(np.asarray(ins["L0_ffn_Bd"], f32).transpose(0, 2, 1))

    s["identf"] = np.eye(128, dtype=f32)
    tri = np.where(np.arange(CH)[:, None] <= np.arange(CH)[None, :], 0.0, -1e5)
    s["mtri"] = np.ascontiguousarray(np.tile(tri, (2, 1)).astype(f32))
    s["mres"] = np.ascontiguousarray(
        np.concatenate([np.zeros((CH, 1), f32), np.ones((CH, CH - 1), f32)], 1))
    selp = np.zeros((32, 64, 128), f32)
    for hp in range(32):
        selp[hp, 2 * hp, 0:64] = 1.0
        selp[hp, 2 * hp + 1, 64:128] = 1.0
    s["selp"] = selp
    i64x = np.zeros((128, 128), f32)
    i64x[0:64, 0:64] = np.eye(64)
    i64x[64:128, 64:128] = np.eye(64)
    s["i64x"] = i64x
    s["ones1"] = np.ones((1, 128), f32)
    sel8 = np.zeros((8, 128), f32)
    for e in range(8):
        sel8[e, 16 * e:16 * e + 16] = 1.0
    s["sel8"] = sel8
    s["zz16"] = np.zeros((RK, DFF), f32)
    m512 = np.ones((CH, TB), f32)
    m512[:, 0::CH] = 0.0
    s["mres512"] = m512
    return s


class B:
    ENGS = ("tensor", "vector", "scalar", "gpsimd", "sync")

    def __init__(self, nc):
        self.nc = nc
        self.q = {e: [] for e in self.ENGS}
        self.cnt = {}

    def on(self, eng, fn, inc=None, n=1):
        if inc is not None:
            self.cnt[inc] = self.cnt.get(inc, 0) + n
        self.q[eng].append((fn, inc, n))

    def dma(self, eng, out, in_, inc):
        self.on(eng, (lambda e, o=out, i=in_: e.dma_start(out=o, in_=i)), inc, 16)
        return self.cnt[inc]

    def wait(self, eng, sem, val=None):
        v = self.cnt.get(sem, 0) if val is None else val
        assert v <= self.cnt.get(sem, 0), "wait on future semaphore value"
        if v > 0:
            self.q[eng].append(((lambda e, s=sem, vv=v: e.wait_ge(s, vv)), None, 0))

    def barrier(self, sems):
        for e in self.ENGS:
            self.q[e].append(((lambda en: en.drain()), None, 0))
            self.on(e, (lambda en: None), sems[e], 1)
        for e in self.ENGS:
            for e2 in self.ENGS:
                if e2 != e:
                    self.wait(e, sems[e2])

    def finalize(self, block):
        for e in self.ENGS:
            ops = list(self.q[e])

            def mk(ops):
                def run(eng):
                    for fn, inc, n in ops:
                        r = fn(eng)
                        if inc is not None:
                            if r is None:
                                eng.sem_inc(inc, n)
                            else:
                                r.then_inc(inc, n)
                return run
            getattr(block, e)(mk(ops))


def build(dump=(), ssd_stage=99):
    nc = bass.Bass("TRN2", target_bir_lowering=False)
    dump = set(dump)

    def din(name, shape, dt=F32):
        return nc.dram_tensor(name, list(shape), dt, kind="ExternalInput").ap()

    x_in = din("x", [L, DM])
    w1c = din("w1c", [NGA1, 128, 16, 128])
    wzt = din("wzt", [16, 128, 4096])
    wot = din("wot", [32, 128, 2048])
    cwdiag_i = din("cwdiag", [34, 128, 4, 128])
    convb_i = din("convb", [128, 34])
    alog_i = din("alog", [NH, 1])
    dvec_i = din("dvec", [NH, 1])
    dtb_i = din("dtb", [NH, 1])
    nw_i = din("nw", [128, 32])
    wgc = din("wgc", [NJ, 128, 16, 128])
    wuc = din("wuc", [NJ, 128, 16, 128])
    wdc = din("wdc", [4, 128, NJ, 512])
    wrt = din("wrt", [16, 128, 8])
    agt_i = din("agt", [16, 128, 128])
    aut_i = din("aut", [16, 128, 128])
    bgt_i = din("bgt", [NE, RK, DFF])
    but_i = din("but", [NE, RK, DFF])
    adt_i = din("adt", [NE, NJ, 128, RK])
    bdt_i = din("bdt", [NE, RK, DM])
    identf_i = din("identf", [128, 128])
    mtri_i = din("mtri", [128, CH])
    mres_i = din("mres", [CH, CH])
    selp_i = din("selp", [32, 64, 128])
    i64x_i = din("i64x", [128, 128])
    ones1_i = din("ones1", [1, 128])
    sel8_i = din("sel8", [8, 128])
    zz16_i = din("zz16", [RK, DFF])
    mres512_i = din("mres512", [CH, TB])

    out_t = nc.dram_tensor("out", [L, DM], F32, kind="ExternalOutput").ap()
    ysc = nc.dram_tensor("ysc", [L, DIN], BF16, kind="Internal").ap()
    ynsc = nc.dram_tensor("ynsc", [L, DIN], BF16, kind="Internal").ap()
    x1sc = nc.dram_tensor("x1sc", [L, DM], BF16, kind="Internal").ap()
    xfsc = nc.dram_tensor("xfsc", [16, 128, 16, 128], BF16, kind="Internal").ap()
    wesc = nc.dram_tensor("wesc", [8, L], BF16, kind="Internal").ap()

    dump_specs = {
        "xn": ([NBLK, 128, 16, TB], BF16),
        "xnt0": ([128, DM], BF16),
        "stat0": ([128, 8], F32),
        "scr0": ([128, DM], F32),
        "dtraw": ([NBLK, CH, TB], F32),
        "raw": ([NBLK, 128, NGX, TB], BF16),
        "cvo": ([NBLK, 128, NGX, TB], BF16),
        "cumA": ([NBLK, CH, TB], F32),
        "dtsp": ([NBLK, CH, TB], F32),
        "wt": ([NBLK, 128, 32, NCHB * CH], BF16),
        "xdtp": ([NBLK, NCHB // 2, 128, DIN], BF16),
        "srun": ([NBLK, 128, DIN], BF16),
        "y": ([L, DIN], BF16),
        "x1": ([L, DM], BF16),
        "yn": ([L, DIN], BF16),
        "sz0": ([128, DIN], BF16),
        "yg0": ([128, DIN], BF16),
        "we0": ([8, 512], BF16),
        "tew0": ([128, 512], BF16),
        "hb0": ([128, 512], BF16),
        "lg0": ([128, 8], F32),
        "xfh0": ([128, 4, 16, 128], BF16),
        "gate0q": ([128, 512], BF16),
        "wet0": ([128, 8], BF16),
        "st0": ([128, 8], F32),
        "wt0": ([128, 32, 256], BF16),
        "sr0": ([128, DIN], BF16),
        "ar0": ([2, 128, DIN], BF16),
        "xd0": ([2, 128, DIN], BF16),
        "vall0": ([128, NCHB, 32], F32),
        "vall": ([NBLK, 128, NCHB, 32], F32),
        "lam": ([NBLK, 128, NCHB, CH], F32),
    }
    dump_t = {k: nc.dram_tensor("d_" + k, list(sh), dt, kind="ExternalOutput").ap()
              for k, (sh, dt) in dump_specs.items() if k in dump}

    b = B(nc)
    ctx = []

    def sb(name, shape, dt):
        t = nc.sbuf_tensor(name, list(shape), dt)
        h = t.__enter__()
        ctx.append(t)
        return h

    def psum(name, shape, dt=F32):
        t = nc.psum_tensor(name, list(shape), dt)
        h = t.__enter__()
        ctx.append(t)
        return h

    def sem(name):
        t = nc.semaphore(name)
        h = t.__enter__()
        ctx.append(t)
        return h

    sLx = sem("sLx")
    sLw = sem("sLw")
    sLs = sem("sLs")
    sT = sem("sT")
    sV = sem("sV")
    sS = sem("sS")
    sbar = {e: sem("bar_" + e) for e in B.ENGS}

    def BAR():
        b.barrier(sbar)

    identb = sb("identb", [128, 128], BF16)
    identf = sb("identf_s", [128, 128], F32)
    convb = sb("convb_s", [128, 34], F32)
    alog = sb("alog_s", [NH, 1], F32)
    ealog = sb("ealog_s", [NH, 1], F32)
    dvec = sb("dvec_s", [NH, 1], F32)
    dtbn = sb("dtbn_s", [NH, 1], F32)
    nw = sb("nw_s", [128, 32], F32)
    mtri = sb("mtri_s", [128, CH], F32)
    mres = sb("mres_s", [CH, CH], F32)
    selp = sb("selp_s", [64, 32, 128], F32)
    i64x = sb("i64x_s", [128, 128], BF16)
    ones1 = sb("ones1_s", [1, 128], F32)
    sel8 = sb("sel8_s", [8, 128], BF16)
    zcol = sb("zcol", [CH, 1], F32)
    drep = sb("drep", [128, NH], BF16)
    drow = sb("drow", [1, NH], F32)

    xnt = sb("xnt", [128, DM], BF16)
    tails = sb("tails", [128, NGX, 3], BF16)
    xtm = sb("xtm", [128, DM], F32)
    scr1 = sb("scr1", [128, 1024], F32)
    stat = sb("stat", [128, 8], F32)
    srun = sb("srun", [128, DIN], BF16)

    AR_ELE = 84000
    UB = sb("UB", [128, AR_ELE], BF16)

    class Arena:
        def __init__(self):
            self.off = 0

        def take(self, shape, dt):
            esz = 2 if dt == BF16 else 4
            n = int(np.prod(shape[1:])) * esz
            off = (self.off + 3) & ~3
            self.off = off + n
            assert self.off <= AR_ELE * 2, f"arena overflow: {self.off}"
            ap = UB[:, off // 2:(off + n) // 2]
            if dt != BF16:
                ap = ap.bitcast(dt)
            if len(shape) == 2:
                pass
            elif len(shape) == 3:
                ap = ap.rearrange("p (a b) -> p a b", a=shape[1])
            elif len(shape) == 4:
                ap = ap.rearrange("p (a b c) -> p a b c", a=shape[1], b=shape[2])
            else:
                raise AssertionError(shape)
            if shape[0] != 128:
                ap = ap[0:shape[0]]
            return ap

    arena = Arena()
    A = arena.take
    xns = A([128, 16, TB], BF16)
    rawr = A([128, 2, TB + 3], BF16)
    dtraw = A([CH, TB], F32)
    cvo = A([128, NGX, TB], BF16)
    w1sb = A([128, 2, 2048], BF16)
    dtsp = A([CH, TB], F32)
    adisc = A([CH, TB], F32)
    cumA = A([CH, TB], F32)
    ends = A([CH, NCHB], F32)
    decay = A([CH, TB], F32)
    sdo = A([CH, TB], F32)
    erow8 = A([1, TB], F32)
    mres512 = A([CH, TB], F32)
    lamrep = A([128, NCHB, CH], F32)
    sdoTc = A([64, NCHB, CH], BF16)
    cumATs = A([128, NCHB, CH], F32)
    vall = A([128, NCHB, 32], F32)
    dtT = A([128, NCHB // 2, CH], BF16)
    decT = A([128, NCHB // 2, CH], BF16)
    wt = A([128, 32, 4 * CH], BF16)
    wtmp = A([128, 4 * CH], F32)
    wexp = A([128, 4 * CH], BF16)
    b2 = A([128, 2, 128], BF16)
    xdt = A([128, DIN], BF16)
    xdd = A([128, DIN], BF16)
    xdv = A([128, DIN], BF16)
    arT = A([128, DIN], BF16)
    arB = A([128, DIN], BF16)
    btok = A([128, 4, 128], BF16)
    srt = A([128, 512], BF16)
    btstage = A([CH, 128], BF16)
    yacc = A([64, DIN], BF16)
    ytile = A([64, DIN], BF16)
    cwg = A([128, 2, 512], BF16)
    a_end = arena.off

    pBig = psum("pBig", [128, 8, 512])
    pA = pBig[:, 0:2, :]
    pT = pBig[:, 2:4, :].bitcast(BF16).rearrange("p a b -> p (a b)")
    pW = pBig[:, 4, :]
    pSc = pBig[:, 5, :]
    pY = pBig[:, 6, :]
    pY2 = pBig[:, 7, :]

    b.dma("sync", identf[:], identf_i, sLx)
    b.dma("gpsimd", identb[:], identf_i, sLw)
    b.dma("sync", convb[:], convb_i, sLx)
    b.dma("sync", alog[:], alog_i, sLx)
    b.dma("sync", dvec[:], dvec_i, sLx)
    b.dma("sync", mtri[:], mtri_i, sLx)
    b.dma("sync", mres[:], mres_i, sLx)
    b.dma("sync", nw[:], nw_i, sLx)
    b.dma("sync", ones1[:], ones1_i, sLx)
    b.dma("gpsimd", sel8[:], sel8_i, sLw)
    b.dma("gpsimd", i64x[:], i64x_i, sLw)
    for hp in range(32):
        b.dma("sync", selp[:, hp, :], selp_i[hp], sLx)
    b.dma("sync", dtbn[:], dtb_i, sLx)
    b.wait("vector", sLx)
    b.wait("vector", sLw)
    b.on("vector", lambda e: e.memset(zcol[:], 0.0))
    b.on("vector", lambda e: e.tensor_scalar(dtbn[:], dtbn[:], -1.0, None, OP.mult), sV)
    b.wait("scalar", sLx)
    b.on("scalar", lambda e: e.activation(ealog[:], alog[:], AF.Exp), sS)
    b.wait("tensor", sLx)
    b.wait("tensor", sLw)
    b.on("tensor", lambda e: e.transpose(pW[0:1, 0:NH], dvec[:], identf[0:NH, 0:NH]), sT)
    b.wait("scalar", sT)
    b.on("scalar", lambda e: e.copy(drow[:], pW[0:1, 0:NH]), sS)
    b.wait("tensor", sS)
    b.on("tensor", lambda e: e.matmul(pY[:, 0:NH], ones1[:], drow[:],
                                      start=True, stop=True), sT)
    b.wait("vector", sT)
    b.on("vector", lambda e: e.tensor_copy(drep[:], pY[:, 0:NH]), sV)
    b.on("vector", lambda e: e.memset(tails[:], 0.0), sV)
    b.on("vector", lambda e: e.memset(btok[:], 0.0), sV)
    b.on("vector", lambda e: e.memset(arT[:], 0.0), sV)
    b.on("vector", lambda e: e.memset(arB[:], 0.0), sV)
    b.on("vector", lambda e: e.memset(srun[:], 0.0), sV)
    b.dma("sync", mres512[:], mres512_i, sLx)
    b.wait("vector", sLx)
    BAR()

    def DUMP(name, dst, src):
        if name not in dump_t:
            return
        for e2 in ("tensor", "vector", "scalar"):
            pass
        b.wait("gpsimd", sT)
        b.wait("gpsimd", sV)
        b.wait("gpsimd", sS)
        b.dma("gpsimd", dst, src, sLw)

    for blk in range(NBLK):
        t0 = blk * TB
        for i in range(4):
            b.wait("sync", sV)
            xrdy = b.dma("sync", xtm[:], x_in[t0 + i * 128: t0 + (i + 1) * 128, :], sLx)
            b.wait("scalar", sLx, xrdy)
            b.wait("vector", sLx, xrdy)
            b.on("scalar", lambda e: e.activation(
                scr1[:], xtm[:, 0:1024], AF.Square), sS)
            b.on("vector", lambda e: e.reduce_sum(
                stat[:, 0:1], xtm[:], axis=AX.X))
            b.wait("vector", sS)
            b.on("vector", lambda e: e.reduce_sum(
                stat[:, 1:2], scr1[:], axis=AX.X), sV)
            b.wait("scalar", sV)
            b.on("scalar", lambda e: e.activation(
                scr1[:], xtm[:, 1024:2048], AF.Square), sS)
            b.wait("vector", sS)
            b.on("vector", lambda e: e.reduce_sum(
                stat[:, 6:7], scr1[:], axis=AX.X))
            b.on("vector", lambda e: e.tensor_scalar(
                stat[:, 2:3], stat[:, 0:1], 1.0 / DM, None, OP.mult))
            b.on("vector", lambda e: e.tensor_tensor(
                stat[:, 1:2], stat[:, 1:2], stat[:, 6:7], OP.add))
            b.on("vector", lambda e: e.tensor_tensor(
                stat[:, 4:5], stat[:, 2:3], stat[:, 2:3], OP.mult))
            b.on("vector", lambda e: e.tensor_scalar(
                stat[:, 3:4], stat[:, 1:2], 1.0 / DM, None, OP.mult))
            b.on("vector", lambda e: e.tensor_copy(scr1[:, 40:41], scr1[:, 41:42]))
            b.on("vector", lambda e: e.tensor_scalar(
                stat[:, 5:6], stat[:, 3:4], stat[:, 4:5], 1e-5,
                OP.subtract, OP.add), sV)
            b.wait("scalar", sV)
            b.on("scalar", lambda e: e.activation(
                stat[:, 6:7], stat[:, 5:6], AF.Sqrt), sS)
            b.wait("vector", sS)
            b.on("vector", lambda e: e.reciprocal(stat[:, 7:8], stat[:, 6:7]))
            b.wait("vector", sT)
            b.on("vector", lambda e: e.tensor_scalar(
                xnt[:], xtm[:], stat[:, 2:3], stat[:, 7:8],
                OP.subtract, OP.mult), sV)
            b.wait("tensor", sV)
            for kg in range(2):
                b.wait("tensor", sS)
                for k8 in range(8):
                    k = kg * 8 + k8
                    b.on("tensor", lambda e, k=k, k8=k8, kg=kg: e.transpose(
                        pT.rearrange("p (g k c) -> p g k c", g=2, k=8)[:, kg, k8, :],
                        xnt[:, k * 128:(k + 1) * 128], identb[:]),
                        sT if k8 == 7 else None)
                b.wait("scalar", sT)
                b.on("scalar", lambda e, kg=kg, i=i: e.copy(
                    xns[:, kg * 8:(kg + 1) * 8, i * 128:(i + 1) * 128],
                    pT.rearrange("p (g k c) -> p g k c", g=2, k=8)[:, kg, :, :]), sS)
        BAR()
        if "xn" in dump_t:
            DUMP("xn", dump_t["xn"][blk], xns[:])

        w1_ready = {}
        cw_ready = {}
        w1_ready[0] = b.dma("gpsimd", w1sb[:, 0, :],
                            w1c[0].rearrange("p a b -> p (a b)"), sLw)
        cw_ready[0] = b.dma("gpsimd", cwg[:, 0, :],
                            cwdiag_i[0].rearrange("p a b -> p (a b)"), sLw)
        w1_ready[1] = b.dma("gpsimd", w1sb[:, 1, :],
                            w1c[1].rearrange("p a b -> p (a b)"), sLw)
        cw_ready[1] = b.dma("gpsimd", cwg[:, 1, :],
                            cwdiag_i[1].rearrange("p a b -> p (a b)"), sLw)
        for g in range(NGA1):
            rb = g % 2
            b.wait("tensor", sLw, w1_ready[g])
            b.wait("tensor", sS)
            for k in range(16):
                b.on("tensor", lambda e, g=g, k=k: e.matmul(
                    pA[:, 0, :], w1sb[:, g % 2, k * 128:(k + 1) * 128],
                    xns[:, k, :], start=(k == 0), stop=(k == 15)),
                    sT if k == 15 else None)
            if g + 2 < NGA1:
                b.wait("gpsimd", sT)
                w1_ready[g + 2] = b.dma(
                    "gpsimd", w1sb[:, g % 2, :],
                    w1c[g + 2].rearrange("p a b -> p (a b)"), sLw)
                if g + 2 < NGX:
                    cw_ready[g + 2] = b.dma(
                        "gpsimd", cwg[:, g % 2, :],
                        cwdiag_i[g + 2].rearrange("p a b -> p (a b)"), sLw)
            b.wait("scalar", sT)
            b.wait("scalar", sV)
            if g < NGX:
                b.on("scalar", lambda e, g=g, rb=rb: e.copy(
                    rawr[:, rb, 3:3 + TB], pA[:, 0, :]), sS)
                b.wait("vector", sS)
                b.on("vector", lambda e, g=g, rb=rb: e.tensor_copy(
                    rawr[:, rb, 0:3], tails[:, g, :]), sV)
                b.wait("tensor", sV)
                b.wait("tensor", sLw, cw_ready[g])
                for j in range(4):
                    b.on("tensor", lambda e, g=g, j=j, rb=rb: e.matmul(
                        pA[:, 1, :], cwg[:, g % 2, j * 128:(j + 1) * 128],
                        rawr[:, rb, j:TB + j],
                        start=(j == 0), stop=(j == 3)), sT if j == 3 else None)
                b.wait("scalar", sT)
                b.on("scalar", lambda e, g=g: e.activation(
                    cvo[:, g, :], pA[:, 1, :], AF.Silu,
                    bias=convb[:, g:g + 1], scale=1.0), sS)
                b.on("vector", lambda e, g=g, rb=rb: e.tensor_copy(
                    tails[:, g, :], rawr[:, rb, TB:TB + 3]), sV)
            else:
                b.on("scalar", lambda e: e.copy(
                    dtraw[:], pA[0:CH, 0, :]), sS)
        BAR()
        if "dtraw" in dump_t:
            DUMP("dtraw", dump_t["dtraw"][blk], dtraw[:])
        if "cvo" in dump_t:
            DUMP("cvo", dump_t["cvo"][blk], cvo[:])

        if ssd_stage < 1:
            continue
        b.wait("scalar", sS)
        b.on("scalar", lambda e: e.activation(
            dtsp[:], dtraw[:], AF.Sigmoid, bias=dtbn[:], scale=-1.0))
        b.on("scalar", lambda e: e.activation(dtsp[:], dtsp[:], AF.Ln), sS)
        b.wait("vector", sS)
        b.on("vector", lambda e: e.tensor_scalar(
            adisc[:], dtsp[:], ealog[:], None, OP.mult))
        b.on("vector", lambda e: e.tensor_scalar(
            dtsp[:], dtsp[:], -1.0, None, OP.mult), sV)
        b.on("vector", lambda e: e.tensor_tensor_scan(
            cumA[:], mres512[:], adisc[:], zcol[:], OP.mult, OP.add), sV)
        b.wait("scalar", sV)
        b.on("scalar", lambda e: e.copy(
            ends[:].unsqueeze(2),
            cumA[:].rearrange("h (c l) -> h c l", c=NCHB)[:, :, CH - 1:CH]), sS)
        b.wait("vector", sS)
        b.on("vector", lambda e: e.tensor_tensor(
            scr1[0:CH, 0:TB].rearrange("h (c l) -> h c l", c=NCHB),
            ends[:].unsqueeze(2).broadcast_to((CH, NCHB, CH)),
            cumA[:].rearrange("h (c l) -> h c l", c=NCHB), OP.subtract), sV)
        b.wait("scalar", sV)
        b.on("scalar", lambda e: e.activation(decay[:], scr1[0:CH, 0:TB], AF.Exp))
        b.on("scalar", lambda e: e.activation(sdo[:], cumA[:], AF.Exp), sS)

        if ssd_stage < 2:
            continue
        b.wait("tensor", sS)
        b.wait("tensor", sV)
        for ci in range(NCHB):
            b.on("tensor", lambda e, ci=ci: e.transpose(
                pW[0:CH, ci * CH:(ci + 1) * CH],
                cumA[:, ci * CH:(ci + 1) * CH], identf[0:CH, 0:CH]),
                sT if ci == NCHB - 1 else None)
        b.wait("scalar", sT)
        b.on("scalar", lambda e: e.copy(
            cumATs[0:CH].rearrange("p c l -> p (c l)"), pW[0:CH]), sS)
        if ssd_stage < 2.2:
            continue
        b.wait("scalar", sS)
        cat_rdy = b.dma("scalar", cumATs[CH:128].rearrange("p c l -> p (c l)"),
                        cumATs[0:CH].rearrange("p c l -> p (c l)"), sLs)
        b.wait("vector", sLs, cat_rdy)
        if ssd_stage < 2.3:
            continue
        b.wait("vector", sS)
        b.on("vector", lambda e: e.tensor_copy(
            vall[0:CH, :, :],
            cumATs.rearrange("p c (hp two) -> p c hp two", two=2)[0:CH, :, :, 0]))
        b.on("vector", lambda e: e.tensor_copy(
            vall[CH:128, :, :],
            cumATs.rearrange("p c (hp two) -> p c hp two", two=2)[CH:128, :, :, 1]),
            sV)
        if ssd_stage < 2.4:
            continue
        b.wait("tensor", sS)
        for ci in range(NCHB):
            b.on("tensor", lambda e, ci=ci: e.transpose(
                pW[0:CH, ci * CH:(ci + 1) * CH],
                sdo[:, ci * CH:(ci + 1) * CH], identf[0:CH, 0:CH]),
                sT if ci == NCHB - 1 else None)
        b.wait("scalar", sT)
        b.on("scalar", lambda e: e.copy(
            sdoTc[:].rearrange("p c l -> p (c l)"), pW[0:CH, :]), sS)
        if ssd_stage < 2.5:
            continue
        b.wait("tensor", sS)
        for pi in range(NCHB // 2):
            b.on("tensor", lambda e, pi=pi: e.transpose(
                pW[:, pi * CH:(pi + 1) * CH],
                dtsp[:, pi * 128:(pi + 1) * 128], identf[0:CH, 0:CH]),
                sT if pi == 3 else None)
        b.wait("scalar", sT)
        b.on("scalar", lambda e: e.copy(
            dtT[:].rearrange("p c l -> p (c l)"), pW[:, 0:256]), sS)
        b.wait("tensor", sS)
        for pi in range(NCHB // 2):
            b.on("tensor", lambda e, pi=pi: e.transpose(
                pW[:, pi * CH:(pi + 1) * CH],
                decay[:, pi * 128:(pi + 1) * 128], identf[0:CH, 0:CH]),
                sT if pi == 3 else None)
        b.wait("scalar", sT)
        b.on("scalar", lambda e: e.copy(
            decT[:].rearrange("p c l -> p (c l)"), pW[:, 0:256]), sS)
        if ssd_stage < 2.6:
            continue
        b.wait("tensor", sS)
        for ci in range(NCHB):
            b.on("tensor", lambda e, ci=ci: e.transpose(
                pW[0:1, ci * CH:ci * CH + CH],
                ends[:, ci:ci + 1], identf[0:CH, 0:CH]),
                sT if ci == NCHB - 1 else None)
        b.wait("scalar", sT)
        b.on("scalar", lambda e: e.copy(erow8[:], pW[0:1, :]), sS)
        b.wait("tensor", sS)
        b.on("tensor", lambda e: e.matmul(
            pW[:], ones1[:], erow8[:], start=True, stop=True), sT)
        b.wait("scalar", sT)
        b.on("scalar", lambda e: e.activation(
            lamrep[:].rearrange("p c l -> p (c l)"), pW[:], AF.Exp), sS)

        if ssd_stage < 3:
            continue
        for wv in range(2):
            for cw4 in range(4):
                ci = wv * 4 + cw4
                bb = ci % 2
                b.wait("vector", sT)
                b.on("vector", lambda e, ci=ci, bb=bb: e.tensor_copy(
                    b2[:, bb, 0:CH], cvo[:, 32, ci * CH:(ci + 1) * CH]))
                b.on("vector", lambda e, ci=ci, bb=bb: e.tensor_copy(
                    b2[:, bb, CH:128], cvo[:, 32, ci * CH:(ci + 1) * CH]), sV)
                b.wait("tensor", sV)
                b.on("tensor", lambda e, cw4=cw4, ci=ci, bb=bb: e.matmul(
                    pSc[:, cw4 * CH:(cw4 + 1) * CH], b2[:, bb, :],
                    cvo[:, 33, ci * CH:(ci + 1) * CH], start=True, stop=True), sT)

            b.wait("tensor", sS)
            for hp in range(32):
                b.wait("tensor", sV)
                b.on("tensor", lambda e, hp=hp, wv=wv: e.matmul(
                    pW[:, 0:256], selp[:, hp, :],
                    cumA[:, wv * 256:(wv + 1) * 256], start=True, stop=True), sT)
                b.wait("vector", sT)
                b.on("vector", lambda e, hp=hp, wv=wv: e.tensor_tensor(
                    wtmp[:].rearrange("p (c l) -> p c l", c=4),
                    pW[:, 0:256].rearrange("p (c l) -> p c l", c=4),
                    vall[:, wv * 4:(wv + 1) * 4, hp:hp + 1]
                    .broadcast_to((128, 4, CH)), OP.subtract), sV)
                b.on("vector", lambda e: e.tensor_tensor(
                    wtmp[:].rearrange("p (c l) -> p c l", c=4),
                    wtmp[:].rearrange("p (c l) -> p c l", c=4),
                    mtri[:].unsqueeze(1).broadcast_to((128, 4, CH)), OP.add), sV)
                b.wait("scalar", sV)
                b.on("scalar", lambda e: e.activation(wexp[:], wtmp[:], AF.Exp), sS)
                b.wait("vector", sS)
                b.on("vector", lambda e, hp=hp: e.tensor_tensor(
                    wt[:, hp, :], wexp[:], pSc[:, 0:256], OP.mult), sV)

            if ssd_stage < 4:
                continue
            if blk == 0 and wv == 0 and "wt0" in dump_t:
                DUMP("wt0", dump_t["wt0"], wt[:])
                DUMP("vall0", dump_t["vall0"], vall[:])
            for cw4 in range(4):
                ci = wv * 4 + cw4
                c_glob = blk * NCHB + ci
                pi = ci // 2
                if ci % 2 == 0:
                    for rnd in range(2):
                        b.wait("tensor", sV)
                        b.wait("tensor", sS)
                        b.wait("vector", sLs)
                        b.wait("vector", sT)
                        for gj in range(16):
                            g = rnd * 16 + gj
                            b.on("tensor", lambda e, g=g, gj=gj, pi=pi: e.transpose(
                                pT.rearrange("p (g c) -> p g c", g=16)[:, gj, :],
                                cvo[:, g, pi * 128:(pi + 1) * 128], identb[:]),
                                sT if gj == 15 else None)
                        b.wait("vector", sT)
                        b.on("vector", lambda e, rnd=rnd, pi=pi: e.tensor_tensor(
                            xdt[:, rnd * 2048:(rnd + 1) * 2048].rearrange(
                                "p (h q) -> p h q", q=CH),
                            pT[:].rearrange("p (h q) -> p h q", q=CH),
                            dtT[:, pi, rnd * 32:(rnd + 1) * 32].unsqueeze(2)
                            .broadcast_to((128, 32, CH)), OP.mult), sV)
                        b.on("vector", lambda e, rnd=rnd: e.tensor_tensor(
                            xdv[:, rnd * 2048:(rnd + 1) * 2048].rearrange(
                                "p (h q) -> p h q", q=CH),
                            pT[:].rearrange("p (h q) -> p h q", q=CH),
                            drep[:, rnd * 32:(rnd + 1) * 32].unsqueeze(2)
                            .broadcast_to((128, 32, CH)), OP.mult), sV)
                    b.on("vector", lambda e, pi=pi: e.tensor_tensor(
                        xdd[:].rearrange("p (h q) -> p h q", q=CH),
                        xdt[:].rearrange("p (h q) -> p h q", q=CH),
                        decT[:, pi, :].unsqueeze(2).broadcast_to((128, NH, CH)),
                        OP.mult), sV)
                b.wait("scalar", sV)
                b.wait("scalar", sT)
                b.dma("scalar", arT[0:CH, :],
                      xdt[(ci % 2) * CH:(ci % 2) * CH + CH, :], sLs)
                ar_rdy = b.dma("scalar", arB[CH:128, :],
                               xdt[(ci % 2) * CH:(ci % 2) * CH + CH, :], sLs)
                bt_i = ci % 4
                b.wait("tensor", sS)
                b.wait("tensor", sV)
                if ci % 2 == 0:
                    b.on("tensor", lambda e, ci=ci: e.transpose(
                        pT[0:CH, 0:128], cvo[:, 32, ci * CH:(ci + 1) * CH],
                        identb[:]), sT)
                    b.wait("scalar", sT)
                    b.on("scalar", lambda e, bt_i=bt_i: e.copy(
                        btok[0:CH, bt_i, :], pT[0:CH, 0:128]), sS)
                else:
                    b.on("tensor", lambda e, ci=ci: e.transpose(
                        pT[0:CH, 0:128], cvo[:, 32, ci * CH:(ci + 1) * CH],
                        identb[:]), sT)
                    b.wait("scalar", sT)
                    b.on("scalar", lambda e: e.copy(
                        btstage[:], pT[0:CH, 0:128]), sS)
                    b.wait("scalar", sS)
                    bt_rdy = b.dma("scalar", btok[CH:128, bt_i, :], btstage[:], sLs)
                    b.wait("tensor", sLs, bt_rdy)

                if ssd_stage < 5:
                    continue
                b.wait("tensor", sLs, ar_rdy)
                b.wait("tensor", sS)
                b.wait("vector", sLx)
                for sl in range(8):
                    b.on("tensor", lambda e, sl=sl, bt_i=bt_i: e.matmul(
                        pSc[:], btok[:, bt_i, :], xdd[:, sl * 512:(sl + 1) * 512],
                        start=True, stop=True), sT)
                    b.wait("vector", sT)
                    b.on("vector", lambda e, ci=ci, sl=sl: e.tensor_tensor(
                        srt[:].rearrange("p (h q) -> p h q", q=CH),
                        srun[:, sl * 512:(sl + 1) * 512].rearrange(
                            "p (h q) -> p h q", q=CH),
                        lamrep[:, ci, sl * 8:(sl + 1) * 8].unsqueeze(2)
                        .broadcast_to((128, 8, CH)), OP.mult), sV)
                    b.on("vector", lambda e, sl=sl: e.tensor_tensor(
                        srun[:, sl * 512:(sl + 1) * 512], srt[:], pSc[:], OP.add),
                        sV)
                    b.wait("tensor", sV)
                    b.on("tensor", lambda e, ci=ci, sl=sl: e.matmul(
                        pY[0:CH, :], cvo[:, 33, ci * CH:(ci + 1) * CH],
                        srun[:, sl * 512:(sl + 1) * 512],
                        start=True, stop=True), sT)
                    b.on("tensor", lambda e, ci=ci, sl=sl: e.matmul(
                        pY2[0:CH, :], i64x[:, (ci % 2) * CH:(ci % 2) * CH + CH],
                        xdv[:, sl * 512:(sl + 1) * 512],
                        start=True, stop=False))
                    for j in range(8):
                        h = sl * 8 + j
                        src_ar = arT if h % 2 == 0 else arB
                        b.on("tensor", lambda e, h=h, j=j, cw4=cw4, src_ar=src_ar:
                             e.matmul(
                                 pY2[0:CH, j * CH:(j + 1) * CH],
                                 wt[:, h // 2, cw4 * CH:(cw4 + 1) * CH],
                                 src_ar[:, h * CH:(h + 1) * CH],
                                 start=False, stop=(j == 7)), sT if j == 7 else None)
                    b.wait("vector", sT)
                    b.on("vector", lambda e, ci=ci, sl=sl: e.tensor_tensor(
                        yacc[:, sl * 512:(sl + 1) * 512].rearrange(
                            "p (h q) -> p h q", q=CH),
                        pY[0:CH, :].rearrange("p (h q) -> p h q", q=CH),
                        sdoTc[0:CH, ci, sl * 8:(sl + 1) * 8].unsqueeze(2)
                        .broadcast_to((CH, 8, CH)), OP.mult), sV)
                    b.on("vector", lambda e, sl=sl: e.tensor_tensor(
                        ytile[:, sl * 512:(sl + 1) * 512],
                        yacc[:, sl * 512:(sl + 1) * 512], pY2[0:CH, :], OP.add), sV)
                    b.wait("tensor", sV)
                b.wait("sync", sV)
                b.dma("sync", ysc[c_glob * CH:(c_glob + 1) * CH, :],
                      ytile[:], sLx)
                if blk == 0 and ci == 0 and "sr0" in dump_t:
                    DUMP("sr0", dump_t["sr0"], srun[:])
                    DUMP("ar0", dump_t["ar0"][0], arT[:])
                    DUMP("ar0", dump_t["ar0"][1], arB[:])
                    DUMP("xd0", dump_t["xd0"][0], xdd[:])
                    DUMP("xd0", dump_t["xd0"][1], xdv[:])
                    b.wait("scalar", sLw)
                    b.wait("vector", sLw)
        BAR()
        if "cumA" in dump_t:
            DUMP("cumA", dump_t["cumA"][blk], cumA[:])
        if "dtsp" in dump_t:
            DUMP("dtsp", dump_t["dtsp"][blk], dtsp[:])
        if "srun" in dump_t:
            DUMP("srun", dump_t["srun"][blk], srun[:])
        if "lam" in dump_t:
            DUMP("lam", dump_t["lam"][blk], lamrep[:])

    if "y" in dump_t:
        b.wait("sync", sLx)
        b.dma("sync", dump_t["y"], ysc, sLx)
        b.wait("sync", sLx)

    a5 = Arena()
    wz = a5.take([128, 16, 4096], BF16)
    ytm = a5.take([128, DIN], BF16)
    sz = a5.take([128, DIN], BF16)
    ygs = a5.take([128, DIN], BF16)
    xnf = a5.take([128, 16, 128], BF16)
    for k in range(16):
        b.dma("gpsimd", wz[:, k, :], wzt[k], sLw)
    for t in range(16):
        b.wait("sync", sV)
        yrdy = b.dma("sync", ytm[:], ysc[t * 128:(t + 1) * 128, :], sLx)
        xrdy = b.dma("sync", xtm[:], x_in[t * 128:(t + 1) * 128, :], sLx)
        b.wait("scalar", sLx, xrdy)
        b.wait("vector", sLx, xrdy)
        b.on("scalar", lambda e: e.activation(scr1[:], xtm[:, 0:1024], AF.Square), sS)
        b.on("vector", lambda e: e.reduce_sum(stat[:, 0:1], xtm[:], axis=AX.X))
        b.wait("vector", sS)
        b.on("vector", lambda e: e.reduce_sum(stat[:, 1:2], scr1[:], axis=AX.X), sV)
        b.wait("scalar", sV)
        b.on("scalar", lambda e: e.activation(scr1[:], xtm[:, 1024:2048], AF.Square), sS)
        b.wait("vector", sS)
        b.on("vector", lambda e: e.reduce_sum(stat[:, 6:7], scr1[:], axis=AX.X))
        b.on("vector", lambda e: e.tensor_scalar(
            stat[:, 2:3], stat[:, 0:1], 1.0 / DM, None, OP.mult))
        b.on("vector", lambda e: e.tensor_tensor(
            stat[:, 1:2], stat[:, 1:2], stat[:, 6:7], OP.add))
        b.on("vector", lambda e: e.tensor_tensor(
            stat[:, 4:5], stat[:, 2:3], stat[:, 2:3], OP.mult))
        b.on("vector", lambda e: e.tensor_scalar(
            stat[:, 3:4], stat[:, 1:2], 1.0 / DM, None, OP.mult))
        b.on("vector", lambda e: e.tensor_copy(scr1[:, 40:41], scr1[:, 41:42]))
        b.on("vector", lambda e: e.tensor_scalar(
            stat[:, 5:6], stat[:, 3:4], stat[:, 4:5], 1e-5,
            OP.subtract, OP.add), sV)
        b.wait("scalar", sV)
        b.on("scalar", lambda e: e.activation(stat[:, 6:7], stat[:, 5:6], AF.Sqrt), sS)
        b.wait("vector", sS)
        b.on("vector", lambda e: e.reciprocal(stat[:, 7:8], stat[:, 6:7]))
        b.wait("vector", sT)
        b.on("vector", lambda e: e.tensor_scalar(
            xnt[:], xtm[:], stat[:, 2:3], stat[:, 7:8], OP.subtract, OP.mult), sV)
        b.wait("tensor", sV)
        b.wait("tensor", sS)
        for k in range(16):
            b.on("tensor", lambda e, k=k: e.transpose(
                pT.rearrange("p (k c) -> p k c", k=16)[:, k, :],
                xnt[:, k * 128:(k + 1) * 128], identb[:]), sT if k == 15 else None)
        b.wait("scalar", sT)
        b.on("scalar", lambda e: e.copy(
            xnf[:], pT.rearrange("p (k c) -> p k c", k=16)), sS)
        b.wait("tensor", sS)
        b.wait("tensor", sLw)
        for sl in range(8):
            bk = sl % 2
            b.wait("tensor", sS)
            for k in range(16):
                b.on("tensor", lambda e, sl=sl, k=k, bk=bk: e.matmul(
                    pA[:, bk, :], xnf[:, k, :], wz[:, k, sl * 512:(sl + 1) * 512],
                    start=(k == 0), stop=(k == 15)), sT if k == 15 else None)
            b.wait("scalar", sT)
            b.on("scalar", lambda e, sl=sl, bk=bk: e.activation(
                sz[:, sl * 512:(sl + 1) * 512], pA[:, bk, :], AF.Silu), sS)
        b.wait("vector", sS)
        b.wait("vector", sLx, yrdy)
        b.on("vector", lambda e: e.tensor_tensor(ygs[:], ytm[:], sz[:], OP.mult), sV)
        if t == 0 and "sz0" in dump_t:
            DUMP("sz0", dump_t["sz0"], sz[:])
            DUMP("yg0", dump_t["yg0"], ygs[:])
            b.wait("scalar", sLw)
            b.wait("vector", sLw)
        for q in range(4):
            b.wait("scalar", sV)
            b.on("scalar", lambda e, q=q: e.activation(
                scr1[:], ygs[:, q * 1024:(q + 1) * 1024], AF.Square), sS)
            b.wait("vector", sS)
            b.on("vector", lambda e, q=q: e.reduce_sum(
                stat[:, q:q + 1], scr1[:], axis=AX.X), sV)
        b.on("vector", lambda e: e.tensor_tensor(
            stat[:, 4:5], stat[:, 0:1], stat[:, 1:2], OP.add))
        b.on("vector", lambda e: e.tensor_tensor(
            stat[:, 5:6], stat[:, 2:3], stat[:, 3:4], OP.add))
        b.on("vector", lambda e: e.tensor_copy(scr1[:, 40:41], scr1[:, 41:42]))
        b.on("vector", lambda e: e.tensor_tensor(
            stat[:, 6:7], stat[:, 4:5], stat[:, 5:6], OP.add))
        b.on("vector", lambda e: e.tensor_copy(scr1[:, 42:43], scr1[:, 43:44]))
        b.on("vector", lambda e: e.tensor_scalar(
            stat[:, 3:4], stat[:, 6:7], 1.0 / DIN, 1e-5, OP.mult, OP.add), sV)
        b.wait("scalar", sV)
        b.on("scalar", lambda e: e.activation(stat[:, 4:5], stat[:, 3:4], AF.Sqrt), sS)
        b.wait("vector", sS)
        b.on("vector", lambda e: e.reciprocal(stat[:, 5:6], stat[:, 4:5]))
        b.on("vector", lambda e: e.tensor_copy(stat[:, 7:8], stat[:, 5:6]))
        b.on("vector", lambda e: e.tensor_scalar(
            ygs[:], ygs[:], stat[:, 5:6], None, OP.mult), sV)
        b.wait("sync", sV)
        b.dma("sync", ynsc[t * 128:(t + 1) * 128, :], ygs[:], sLx)
    BAR()

    a5b = Arena()
    wo = a5b.take([128, 32, 2048], BF16)
    ynf = a5b.take([128, 32, 128], BF16)
    yntm = a5b.take([128, DIN], BF16)
    x1t = a5b.take([128, DM], F32)
    xn2f = a5b.take([128, DM], F32)
    xnf32 = yntm.bitcast(F32).rearrange("p (a c) -> p a c", a=16)
    wrtf = a5b.take([128, 16, 8], F32)
    lgf2 = a5b.take([128, 64], F32)
    wetile2 = a5b.take([128, 8], BF16)
    westg = a5b.take([8, 128], BF16)
    for k in range(32):
        b.dma("gpsimd", wo[:, k, :], wot[k], sLw)
    for k in range(16):
        b.dma("sync", wrtf[:, k, :], wrt[k], sLx)
    b.wait("vector", sLx)
    b.on("vector", lambda e: e.memset(lgf2[:, 8:64], -1e30), sV)
    b.wait("vector", sLw)
    for k in range(32):
        b.on("vector", lambda e, k=k: e.tensor_scalar(
            wo[:, k, :], wo[:, k, :], nw[:, k:k + 1], None, OP.mult),
            sV if k == 31 else None)
    for t in range(16):
        b.wait("sync", sV)
        b.wait("sync", sLs)
        b.wait("scalar", sLs)
        yrdy = b.dma("sync", yntm[:], ynsc[t * 128:(t + 1) * 128, :], sLx)
        xrdy = b.dma("sync", xtm[:], x_in[t * 128:(t + 1) * 128, :], sLx)
        b.wait("tensor", sLx, yrdy)
        b.wait("tensor", sV)
        b.wait("tensor", sS)
        for rnd in range(2):
            for k8 in range(16):
                k = rnd * 16 + k8
                b.on("tensor", lambda e, k=k, k8=k8: e.transpose(
                    pT.rearrange("p (k c) -> p k c", k=16)[:, k8, :],
                    yntm[:, k * 128:(k + 1) * 128], identb[:]),
                    sT if k8 == 15 else None)
            b.wait("scalar", sT)
            b.on("scalar", lambda e, rnd=rnd: e.copy(
                ynf[:, rnd * 16:(rnd + 1) * 16, :],
                pT.rearrange("p (k c) -> p k c", k=16)), sS)
            b.wait("tensor", sS)
        b.wait("vector", sLx, xrdy)
        for sl in range(4):
            bk = sl % 2
            b.wait("tensor", sV)
            for k in range(32):
                b.on("tensor", lambda e, sl=sl, k=k, bk=bk: e.matmul(
                    pA[:, bk, :], ynf[:, k, :], wo[:, k, sl * 512:(sl + 1) * 512],
                    start=(k == 0), stop=(k == 31)), sT if k == 31 else None)
            b.wait("vector", sT)
            b.on("vector", lambda e, sl=sl, bk=bk: e.tensor_tensor(
                x1t[:, sl * 512:(sl + 1) * 512], pA[:, bk, :],
                xtm[:, sl * 512:(sl + 1) * 512], OP.add), sV)
        b.wait("gpsimd", sV)
        b.dma("gpsimd", x1sc[t * 128:(t + 1) * 128, :], x1t[:], sLw)
        b.wait("scalar", sV)
        b.on("scalar", lambda e: e.activation(scr1[:], x1t[:, 0:1024], AF.Square), sS)
        b.on("vector", lambda e: e.reduce_sum(stat[:, 0:1], x1t[:], axis=AX.X))
        b.wait("vector", sS)
        b.on("vector", lambda e: e.reduce_sum(stat[:, 1:2], scr1[:], axis=AX.X), sV)
        b.wait("scalar", sV)
        b.on("scalar", lambda e: e.activation(scr1[:], x1t[:, 1024:2048], AF.Square), sS)
        b.wait("vector", sS)
        b.on("vector", lambda e: e.reduce_sum(stat[:, 6:7], scr1[:], axis=AX.X))
        b.on("vector", lambda e: e.tensor_scalar(
            stat[:, 2:3], stat[:, 0:1], 1.0 / DM, None, OP.mult))
        b.on("vector", lambda e: e.tensor_tensor(
            stat[:, 1:2], stat[:, 1:2], stat[:, 6:7], OP.add))
        b.on("vector", lambda e: e.tensor_tensor(
            stat[:, 4:5], stat[:, 2:3], stat[:, 2:3], OP.mult))
        b.on("vector", lambda e: e.tensor_scalar(
            stat[:, 3:4], stat[:, 1:2], 1.0 / DM, None, OP.mult))
        b.on("vector", lambda e: e.tensor_copy(scr1[:, 40:41], scr1[:, 41:42]))
        b.on("vector", lambda e: e.tensor_scalar(
            stat[:, 5:6], stat[:, 3:4], stat[:, 4:5], 1e-5,
            OP.subtract, OP.add), sV)
        b.wait("scalar", sV)
        b.on("scalar", lambda e: e.activation(stat[:, 6:7], stat[:, 5:6], AF.Sqrt), sS)
        b.wait("vector", sS)
        b.on("vector", lambda e: e.reciprocal(stat[:, 7:8], stat[:, 6:7]))
        b.wait("vector", sT)
        b.on("vector", lambda e: e.tensor_scalar(
            xnt[:], x1t[:], stat[:, 2:3], stat[:, 7:8], OP.subtract, OP.mult))
        b.on("vector", lambda e: e.tensor_scalar(
            xn2f[:], x1t[:], stat[:, 2:3], stat[:, 7:8], OP.subtract, OP.mult), sV)
        b.wait("tensor", sV)
        b.wait("tensor", sS)
        for k in range(16):
            b.on("tensor", lambda e, k=k: e.transpose(
                pT.rearrange("p (k c) -> p k c", k=16)[:, k, :],
                xnt[:, k * 128:(k + 1) * 128], identb[:]), sT if k == 15 else None)
        b.wait("scalar", sT)
        b.on("scalar", lambda e: e.copy(
            ynf[:, 0:16, :], pT.rearrange("p (k c) -> p k c", k=16)), sS)
        b.wait("scalar", sS)
        b.dma("scalar", xfsc[t], ynf[:, 0:16, :], sLs)
        for rnd in range(4):
            b.wait("tensor", sS)
            for k4 in range(4):
                k = rnd * 4 + k4
                b.on("tensor", lambda e, k=k, k4=k4: e.transpose(
                    pW.rearrange("p (a c) -> p a c", a=4)[:, k4, :],
                    xn2f[:, k * 128:(k + 1) * 128], identf[:]),
                    sT if k4 == 3 else None)
            b.wait("scalar", sT)
            b.on("scalar", lambda e, rnd=rnd: e.copy(
                xnf32[:, rnd * 4:(rnd + 1) * 4, :],
                pW.rearrange("p (a c) -> p a c", a=4)), sS)
        b.wait("tensor", sS)
        for k in range(16):
            b.on("tensor", lambda e, k=k: e.matmul(
                pSc[:, 0:8], xnf32[:, k, :], wrtf[:, k, :],
                start=(k == 0), stop=(k == 15)), sT if k == 15 else None)
        b.wait("scalar", sT)
        b.on("scalar", lambda e: e.copy(lgf2[:, 0:8], pSc[:, 0:8]), sS)
        b.wait("vector", sS)
        DR = lambda e: e.drain()
        b.on("vector", lambda e: e.reduce_max(stat[:, 0:1], lgf2[:], axis=AX.X))
        b.on("vector", DR)
        b.on("vector", lambda e: e.tensor_scalar(
            scr1[:, 64:128], lgf2[:], stat[:, 0:1], None, OP.is_equal))
        b.on("vector", DR)
        b.on("vector", lambda e: e.scalar_tensor_tensor(
            scr1[:, 128:192], scr1[:, 64:128], -1e30, lgf2[:], OP.mult, OP.add))
        b.on("vector", DR)
        b.on("vector", lambda e: e.tensor_scalar(
            stat[:, 1:2], stat[:, 0:1], -1.0, None, OP.mult))
        b.on("vector", DR)
        b.on("vector", lambda e: e.reduce_max(stat[:, 2:3], scr1[:, 128:192],
                                              axis=AX.X), sV)
        b.wait("scalar", sV)
        b.on("scalar", lambda e: e.activation(
            scr1[:, 192:256], lgf2[:], AF.Exp, bias=stat[:, 1:2], scale=1.0), sS)
        b.wait("vector", sS)
        b.on("vector", lambda e: e.reduce_sum(stat[:, 3:4], scr1[:, 192:256],
                                              axis=AX.X))
        b.on("vector", DR)
        b.on("vector", lambda e: e.tensor_scalar(
            scr1[:, 256:320], lgf2[:], stat[:, 2:3], None, OP.is_ge))
        b.on("vector", DR)
        b.on("vector", lambda e: e.tensor_tensor(
            scr1[:, 320:384], scr1[:, 192:256], scr1[:, 256:320], OP.mult))
        b.on("vector", DR)
        b.on("vector", lambda e: e.reciprocal(stat[:, 4:5], stat[:, 3:4]))
        b.on("vector", DR)
        b.on("vector", lambda e: e.tensor_scalar(
            scr1[:, 384:448], scr1[:, 320:384], stat[:, 4:5], None, OP.mult))
        b.on("vector", DR)
        b.on("vector", lambda e: e.tensor_copy(wetile2[:], scr1[:, 384:392]), sV)
        b.wait("tensor", sV)
        b.wait("tensor", sS)
        b.on("tensor", lambda e: e.transpose(
            pT[0:8, 0:128], wetile2[:], identb[:]), sT)
        b.wait("scalar", sT)
        b.on("scalar", lambda e: e.copy(westg[:], pT[0:8, 0:128]), sS)
        b.wait("scalar", sS)
        b.dma("scalar", wesc[:, t * 128:(t + 1) * 128], westg[:], sLs)
    BAR()

    HT = 512
    for hf in range(4):
        fa = Arena()
        h_base = fa.take([128, NJ, HT], BF16)
        bdt_all = fa.take([128, DM], BF16)
        t_ewP = fa.take([128, HT], BF16)
        wd_mark = fa.off
        xfh = fa.take([128, 4, 16, 128], BF16)
        bgt_all = fa.take([128, DFF], BF16)
        but_all = fa.take([128, DFF], BF16)
        adt_all = fa.take([128, NE, NJ * RK], BF16)
        pmg_all = fa.take([128, NE, HT], BF16)
        pmu_all = fa.take([128, NE, HT], BF16)
        wgsb = fa.take([128, 2, 2048], BF16)
        wusb = fa.take([128, 2, 2048], BF16)
        agt_sb = fa.take([128, 16, 128], BF16)
        aut_sb = fa.take([128, 16, 128], BF16)
        wrt_sb = fa.take([128, 16, 8], BF16)
        weT = fa.take([8, HT], BF16)
        wetile = fa.take([128, 8], BF16)
        gate_sb = fa.take([128, HT], BF16)
        up_sb = fa.take([128, HT], BF16)
        sbase = fa.take([128, HT], BF16)
        sgin = fa.take([128, HT], BF16)
        sutile = fa.take([128, HT], BF16)
        h_e = fa.take([128, HT], BF16)
        weRsb = fa.take([128, HT], BF16)
        lgf = fa.take([128, 8], F32)
        fa_wd = Arena()
        fa_wd.off = wd_mark
        wdcol = fa_wd.take([128, NJ, 512], BF16)
        x1tile = fa_wd.take([128, DM], BF16)
        outsb = fa_wd.take([128, 2, 512], F32)

        b.wait("gpsimd", sT)
        b.wait("gpsimd", sV)
        b.wait("gpsimd", sS)
        b.wait("sync", sLs)
        b.wait("sync", sLw)
        for tt in range(4):
            b.dma("sync", xfh[:, tt, :, :].rearrange("p a b -> p (a b)"),
                  xfsc[hf * 4 + tt], sLx)
        if hf == 0:
            pass
        b.on("vector", lambda e: e.memset(bgt_all[:], 0.0))
        b.on("vector", lambda e: e.memset(but_all[:], 0.0))
        b.on("vector", lambda e: e.memset(bdt_all[:], 0.0))
        b.on("vector", lambda e: e.memset(pmg_all[:], 0.0))
        b.on("vector", lambda e: e.memset(pmu_all[:], 0.0), sV)
        b.wait("gpsimd", sV)
        for e_ in range(NE):
            b.dma("gpsimd", bgt_all[16 * e_:16 * e_ + 16, :], bgt_i[e_], sLw)
            b.dma("gpsimd", but_all[16 * e_:16 * e_ + 16, :], but_i[e_], sLw)
            b.dma("gpsimd", bdt_all[16 * e_:16 * e_ + 16, :], bdt_i[e_], sLw)
            b.dma("gpsimd",
                  adt_all[:, e_, :].rearrange("p (a b) -> p a b", a=NJ),
                  adt_i[e_].transpose([1, 0, 2]), sLw)
        for k in range(16):
            b.dma("gpsimd", agt_sb[:, k, :], agt_i[k], sLw)
            b.dma("gpsimd", aut_sb[:, k, :], aut_i[k], sLw)
            b.dma("gpsimd", wrt_sb[:, k, :], wrt[k], sLw)
        b.wait("tensor", sLx)
        b.wait("tensor", sLw)

        if hf == 0 and "xfh0" in dump_t:
            b.wait("gpsimd", sLx)
            b.dma("gpsimd", dump_t["xfh0"], xfh[:].rearrange("p a b c -> p (a b c)"), sLw)
            b.wait("sync", sLw)
        for asb, pm in ((agt_sb, pmg_all), (aut_sb, pmu_all)):
            b.wait("tensor", sS)
            for k in range(16):
                b.on("tensor", lambda e, k=k, asb=asb: e.matmul(
                    pBig[:, 0, :].rearrange("p (a c) -> p a c", a=4),
                    asb[:, k, :], xfh[:, 0:4, k, :],
                    start=(k == 0), stop=(k == 15)), sT if k == 15 else None)
            b.wait("scalar", sT)
            b.on("scalar", lambda e: e.copy(sgin[:], pBig[:, 0, :]), sS)
            b.wait("vector", sS)
            b.on("vector", lambda e: e.tensor_scalar(
                sgin[:], sgin[:], SCALING, None, OP.mult), sV)
            b.wait("scalar", sV)
            for e_ in range(NE):
                b.dma("scalar", pm[16 * e_:16 * e_ + 16, e_, :],
                      sgin[16 * e_:16 * e_ + 16, :], sLs)
            b.wait("tensor", sLs)
            b.wait("vector", sLs)

        b.wait("sync", sS)
        b.dma("sync", weT[:], wesc[:, hf * HT:(hf + 1) * HT], sLx)
        b.wait("tensor", sLx)

        if hf == 0 and "we0" in dump_t:
            DUMP("we0", dump_t["we0"], weT[:])
            b.wait("scalar", sLw)
        wg_ready = {}
        wg_ready[0] = b.dma("gpsimd", wgsb[:, 0, :],
                            wgc[0].rearrange("p a b -> p (a b)"), sLw)
        b.dma("gpsimd", wusb[:, 0, :], wuc[0].rearrange("p a b -> p (a b)"), sLw)
        wg_ready[1] = b.dma("gpsimd", wgsb[:, 1, :],
                            wgc[1].rearrange("p a b -> p (a b)"), sLw)
        b.dma("gpsimd", wusb[:, 1, :], wuc[1].rearrange("p a b -> p (a b)"), sLw)
        for j in range(NJ):
            jb = j % 2
            b.wait("tensor", sLw, wg_ready[j])
            b.wait("tensor", sS)
            b.wait("tensor", sV)
            for k in range(16):
                b.on("tensor", lambda e, k=k, jb=jb: e.matmul(
                    pBig[:, 0, :].rearrange("p (a c) -> p a c", a=4),
                    wgsb[:, jb, k * 128:(k + 1) * 128], xfh[:, 0:4, k, :],
                    start=(k == 0), stop=(k == 15)))
            for k in range(16):
                b.on("tensor", lambda e, k=k, jb=jb: e.matmul(
                    pBig[:, 1, :].rearrange("p (a c) -> p a c", a=4),
                    wusb[:, jb, k * 128:(k + 1) * 128], xfh[:, 0:4, k, :],
                    start=(k == 0), stop=(k == 15)), sT if k == 15 else None)
            if j + 2 < NJ:
                b.wait("gpsimd", sT)
                wg_ready[j + 2] = b.dma(
                    "gpsimd", wgsb[:, j % 2, :],
                    wgc[j + 2].rearrange("p a b -> p (a b)"), sLw)
                b.dma("gpsimd", wusb[:, j % 2, :],
                      wuc[j + 2].rearrange("p a b -> p (a b)"), sLw)
            b.wait("scalar", sT)
            b.wait("scalar", sV)
            b.on("scalar", lambda e: e.copy(gate_sb[:], pBig[:, 0, :]))
            b.on("scalar", lambda e: e.copy(up_sb[:], pBig[:, 1, :]))
            b.on("scalar", lambda e: e.activation(
                sbase[:], pBig[:, 0, :], AF.Silu), sS)
            b.wait("vector", sS)
            b.on("vector", lambda e, j=j: e.tensor_tensor(
                h_base[:, j, :], sbase[:], up_sb[:], OP.mult), sV)
            if hf == 0 and j == 0 and "gate0q" in dump_t:
                DUMP("gate0q", dump_t["gate0q"], gate_sb[:])
                b.wait("scalar", sLw)
            for e_ in range(NE):
                b.wait("tensor", sV)
                b.on("tensor", lambda e, j=j, e_=e_: e.matmul(
                    pBig[:, 2, :], bgt_all[:, j * 128:(j + 1) * 128],
                    pmg_all[:, e_, :], start=True, stop=True))
                b.on("tensor", lambda e, j=j, e_=e_: e.matmul(
                    pBig[:, 3, :], but_all[:, j * 128:(j + 1) * 128],
                    pmu_all[:, e_, :], start=True, stop=True), sT)
                b.wait("vector", sT)
                b.on("vector", lambda e: e.tensor_tensor(
                    sgin[:], gate_sb[:], pBig[:, 2, :], OP.add), sV)
                b.wait("scalar", sV)
                b.on("scalar", lambda e: e.activation(sgin[:], sgin[:], AF.Silu), sS)
                b.on("vector", lambda e: e.tensor_tensor(
                    sutile[:], up_sb[:], pBig[:, 3, :], OP.add), sV)
                b.wait("vector", sS)
                b.on("vector", lambda e: e.tensor_tensor(
                    h_e[:], sgin[:], sutile[:], OP.mult), sV)
                b.wait("tensor", sV)
                for ts4 in range(4):
                    b.on("tensor", lambda e, j=j, e_=e_, ts4=ts4: e.matmul(
                        pBig[:, 4, ts4 * 128 + 16 * e_: ts4 * 128 + 16 * e_ + 16],
                        h_e[:, ts4 * 128:(ts4 + 1) * 128],
                        adt_all[:, e_, j * RK:(j + 1) * RK],
                        start=(j == 0 and e_ == 0 and ts4 == 0),
                        stop=(j == NJ - 1 and ts4 == 3)), sT if ts4 == 3 else None)
        BAR()

        b.wait("scalar", sT)
        b.on("scalar", lambda e: e.copy(sutile[:], pBig[:, 4, :]), sS)
        b.wait("tensor", sS)
        for ts4 in range(4):
            b.on("tensor", lambda e, ts4=ts4: e.transpose(
                pT.rearrange("p (a c) -> p a c", a=16)[:, ts4, :],
                sutile[:, ts4 * 128:(ts4 + 1) * 128], identb[:]),
                sT if ts4 == 3 else None)
        b.on("tensor", lambda e: e.matmul(
            pBig[:, 5, :], sel8[:], weT[:], start=True, stop=True), sT)
        b.wait("scalar", sT)
        b.on("scalar", lambda e: e.copy(
            sgin[:].rearrange("p (a c) -> p a c", a=4),
            pT.rearrange("p (a c) -> p a c", a=16)[:, 0:4, :]), sS)
        b.on("scalar", lambda e: e.copy(weRsb[:], pBig[:, 5, :]), sS)
        b.wait("vector", sS)
        b.on("vector", lambda e: e.scalar_tensor_tensor(
            t_ewP[:], sgin[:], SCALING, weRsb[:], OP.mult, OP.mult), sV)
        if hf == 0 and "tew0" in dump_t:
            DUMP("tew0", dump_t["tew0"], t_ewP[:])
            DUMP("hb0", dump_t["hb0"], h_base[:, 0, :])
            b.wait("scalar", sLw)
            b.wait("vector", sLw)
        BAR()

        for dmq in range(4):
            b.wait("gpsimd", sT)
            wd_rdy = b.dma("gpsimd", wdcol[:].rearrange("p a b -> p (a b)"),
                           wdc[dmq].rearrange("p a b -> p (a b)"), sLw)
            b.wait("tensor", sLw, wd_rdy)
            for tq in range(4):
                t_abs = hf * 4 + tq
                b.wait("sync", sT)
                x1_rdy = b.dma("sync", x1tile[:],
                               x1sc[t_abs * 128:(t_abs + 1) * 128, :], sLx)
                b.wait("tensor", sLx, x1_rdy)
                bk = tq % 2
                b.wait("tensor", sS)
                for j in range(NJ):
                    b.on("tensor", lambda e, j=j, tq=tq, bk=bk: e.matmul(
                        pA[:, bk, :], h_base[:, j, tq * 128:(tq + 1) * 128],
                        wdcol[:, j, :], start=(j == 0), stop=False))
                b.on("tensor", lambda e, tq=tq, bk=bk, dmq=dmq: e.matmul(
                    pA[:, bk, :], t_ewP[:, tq * 128:(tq + 1) * 128],
                    bdt_all[:, dmq * 512:(dmq + 1) * 512],
                    start=False, stop=False))
                b.on("tensor", lambda e, tq=tq, bk=bk, dmq=dmq: e.matmul(
                    pA[:, bk, :], identb[:],
                    x1tile[:, dmq * 512:(dmq + 1) * 512],
                    start=False, stop=True), sT)
                b.wait("scalar", sT)
                b.wait("scalar", sLx)
                b.on("scalar", lambda e, tq=tq, bk=bk: e.copy(
                    outsb[:, tq % 2, :], pA[:, bk, :]), sS)
                b.wait("sync", sS)
                b.dma("sync",
                      out_t[t_abs * 128:(t_abs + 1) * 128,
                            dmq * 512:(dmq + 1) * 512],
                      outsb[:, tq % 2, :], sLx)
        BAR()
    if "x1" in dump_t:
        b.dma("sync", dump_t["x1"], x1sc, sLx)
    if "yn" in dump_t:
        b.dma("sync", dump_t["yn"], ynsc, sLx)
    b.wait("sync", sLx)

    with nc.Block() as block:
        b.finalize(block)
    return nc


def kernel(**inputs):
    shared = _stage_shared(inputs)
    x = np.asarray(inputs["L0_attn_x"], np.float32)
    nc = build()
    in_maps = []
    for c in range(8):
        m = dict(shared)
        m["x"] = np.ascontiguousarray(x[c])
        in_maps.append(m)
    res = run_bass_kernel_spmd(nc, in_maps, core_ids=list(range(8)))
    return np.stack([np.asarray(r["out"], np.float32) for r in res.results])
